# version 07
# baseline (speedup 1.0000x reference)
import numpy as np
import ml_dtypes
from contextlib import ExitStack

import concourse.bass as bass
import concourse.mybir as mybir
from concourse.bass_utils import run_bass_kernel_spmd
from concourse.alu_op_type import AluOpType

F32 = mybir.dt.float32
BF16 = mybir.dt.bfloat16
I32 = mybir.dt.int32
AF = mybir.ActivationFunctionType
AX = mybir.AxisListType
AP = bass.AP

T = 4096
C = 1024
NH = 16
FFI = 2816
NIC = 22
ISH = 384
TCH = 512
NCH = 8
NSC = 2
SCW = 2048
FROW = 256
FOFF = 512
FHSZ = T * FROW + 2 * FOFF
AG_QG = 0
AG_QU = 1024 * 384
AG_QD = 2 * 1024 * 384
AG_SZ = AG_QD + FFI * 128
NCORE = 8


class Prog:
    def __init__(self):
        self.ops = []
        self.count = {}

    def cnt(self, sem):
        return self.count.get(sem, 0)

    def add(self, eng, fn, waits=(), inc=None, amt=1, inc2=None, amt2=1):
        incs = []
        for s, a in ((inc, amt), (inc2, amt2)):
            if s is not None:
                self.count[s] = self.count.get(s, 0) + a
                incs.append((s, a))
        self.ops.append((eng, fn, [w for w in waits if w is not None], incs))
        return (inc, self.count[inc]) if inc else None

    def dma(self, eng, fn, waits=(), inc=None):
        return self.add(eng, fn, waits, inc, amt=16)

    def mark(self, name):
        if not hasattr(self, "marks"):
            self.marks = []
        self.marks.append((name, len(self.ops)))

    def check(self):
        for eng, fn, waits, incs in self.ops:
            for s, v in waits:
                assert self.count.get(s, 0) >= v, \
                    f"wait {s}>={v} unreachable (max {self.count.get(s, 0)})"

    def emit(self, block, sems):
        def mk(name):
            def body(eng):
                for e, fn, waits, incs in self.ops:
                    if e != name:
                        continue
                    for s, v in waits:
                        eng.wait_ge(sems[s], v)
                    inst = fn(eng)
                    if incs:
                        inst.then_inc(sems[incs[0][0]], incs[0][1])
                        for s, a in incs[1:]:
                            eng.nop().then_inc(sems[s], a)
            return body
        block.sync(mk("sync"))
        block.vector(mk("vector"))
        block.scalar(mk("scalar"))
        block.tensor(mk("tensor"))
        block.gpsimd(mk("gpsimd"))


def build(debug=(), limit=None):
    nc = bass.Bass("TRN2", target_bir_lowering=False)
    bfe = ml_dtypes.bfloat16

    def inp(name, shape):
        return nc.dram_tensor(name, list(shape), F32, kind="ExternalInput")

    x_in = inp("x", [T, C])
    wdisp_in = inp("wdisp", [256, C])
    wval_in = inp("wval", [C, C])
    relpos_in = inp("relpos", [64, 16])
    sfw_in = inp("sfw", [32, 16])
    sfb_in = inp("sfb", [32])
    ppw_in = inp("ppw", [16, 16])
    wbond_in = inp("wbond", [1, 16])
    wdmg_in = inp("wdmg", [1, 16])
    dmgb_in = inp("dmgb", [1])
    wcproj_in = inp("wcproj", [C, C])
    clipx_in = inp("clip_x", [1])
    clipd_in = inp("clip_dn", [1])
    wg_in = inp("wg_sh", [ISH, C])
    wu_in = inp("wu_sh", [ISH, C])
    wd_in = inp("wd_sh", [128, FFI])

    out_t = nc.dram_tensor("out", [T, C], F32, kind="ExternalOutput")
    dbg = {}
    def dbgout(name, shape, dtype=F32):
        if name in debug:
            dbg[name] = nc.dram_tensor("dbg_" + name, list(shape), dtype,
                                       kind="ExternalOutput")
        return dbg.get(name)

    xbf = nc.dram_tensor("xbf", [T * C], BF16)
    wvbf = nc.dram_tensor("wvbf", [C * C], BF16)
    wcbf = nc.dram_tensor("wcbf", [C * C], BF16)
    fbuf = nc.dram_tensor("fbuf", [NH * FHSZ], BF16)
    obf = nc.dram_tensor("obf", [C * T], BF16)
    r1d = nc.dram_tensor("r1d", [C * T], BF16)
    h0d = nc.dram_tensor("h0d", [NIC * 128 * T], BF16)
    agw_in = nc.dram_tensor("agw_in", [AG_SZ], BF16)
    agw_out = nc.dram_tensor("agw_out", [NCORE * AG_SZ], BF16)
    ags_in = nc.dram_tensor("ags_in", [1024], BF16)
    ags_out = nc.dram_tensor("ags_out", [NCORE * 1024], BF16)
    mx1_in = nc.dram_tensor("mx1_in", [1024], F32)
    mx1_out = nc.dram_tensor("mx1_out", [1024], F32)
    mx2_in = nc.dram_tensor("mx2_in", [1024], F32)
    mx2_out = nc.dram_tensor("mx2_out", [1024], F32)

    identD = nc.inline_tensor(np.eye(128, dtype=bfe), "identD")
    identFD = nc.inline_tensor(np.eye(128, dtype=np.float32), "identFD")
    p_np = np.zeros((128, 64), dtype=bfe)
    for m in range(64):
        p_np[64 + m, m] = 1
    prelD = nc.inline_tensor(p_np, "prelD")
    ind4_np = np.zeros((64, 4), dtype=bfe)
    for p in range(64):
        ind4_np[p, p // 16] = 1
    ind4D = nc.inline_tensor(ind4_np, "ind4D")
    sel4D = nc.inline_tensor(np.ascontiguousarray(ind4_np.T), "sel4D")
    onesFD = nc.inline_tensor(np.ones((1, 128), dtype=np.float32), "onesFD")
    selb_np = np.zeros((16, 8 * 128), dtype=bfe)
    for h in range(4):
        for f in range(16):
            selb_np[f, h * 128 + 32 * h + f] = 1
            selb_np[f, (4 + h) * 128 + 32 * h + 16 + f] = 1
    selbD = nc.inline_tensor(selb_np, "selbD")
    rep32_np = np.zeros((32, 128), dtype=np.float32)
    for k in range(32):
        for h in range(4):
            rep32_np[k, 32 * h + k] = 1
    rep32D = nc.inline_tensor(rep32_np, "rep32D")
    rep16_np = np.zeros((16, 128), dtype=np.float32)
    for f in range(16):
        for h in range(4):
            rep16_np[f, 32 * h + f] = 1
    rep16D = nc.inline_tensor(rep16_np, "rep16D")
    cm_np = np.zeros((64, 4 * TCH), dtype=bfe)
    for wg in range(4):
        for p in range(64):
            w = 16 * wg + (p % 16)
            cm_np[p, wg * TCH + max(0, 63 - w):(wg + 1) * TCH] = 1
    cmaskD = nc.inline_tensor(cm_np, "cmaskD")

    st = ExitStack()
    SB = lambda n, s, d=BF16: st.enter_context(nc.sbuf_tensor(n, list(s), d))
    PT = lambda n, s, d=F32: st.enter_context(nc.psum_tensor(n, list(s), d))

    wcombT = SB("wcombT", [128, 8 * 512])
    wvT = SB("wvT", [128, 8 * 1024])
    wcT2 = SB("wcT2", [128, 2 * 1024])
    sqbuf = SB("sqbuf", [128, 160])
    sqabs = SB("sqabs", [128, 64])
    ctile = SB("ctile", [128, 64], F32)
    cmaxc = SB("cmaxc", [128, 1], F32)
    ident = SB("ident_s", [128, 128])
    identf = SB("identf_s", [128, 128], F32)
    prel = SB("prel_s", [128, 64])
    ind4 = SB("ind4_s", [64, 4])
    sel4 = SB("sel4_s", [4, 64])
    onesf = SB("onesf_s", [1, 128], F32)
    onescolb = SB("onescolb_s", [128, 1])
    onesrowb = SB("onesrowb_s", [1, 128])
    cmask = SB("cmask_s", [64, 4 * TCH])
    selbs = SB("selbs_s", [16, 8 * 128])
    rep32s = SB("rep32_s", [32, 128], F32)
    rep16s = SB("rep16_s", [16, 128], F32)
    negb = SB("negb", [64, 4 * NSC], F32)
    dmgbc = SB("dmgbc", [64, 1], F32)
    zerot = SB("zerot", [128, 512])
    xa1 = SB("xa1", [128, 2 * TCH])
    xa2 = SB("xa2", [128, 4 * TCH])
    xur = SB("xur", [128, 2 * TCH])
    dispF = SB("dispF", [128, 4 * (64 + SCW)])
    valr = SB("valr", [128, 5 * 1024])
    gout = SB("gout", [128, 2 * SCW])
    dsl = SB("dsl", [128, 2 * TCH])
    e1t = SB("e1t", [64, TCH])
    e2t = SB("e2t", [64, TCH])
    etile = SB("etile", [64, 16 * TCH])
    rrow = SB("rrow", [4, TCH], F32)
    rrowb = SB("rrowb", [4, TCH])
    rbev = SB("rbev", [64, TCH])
    ent = SB("ent", [128, 2 * 256])
    ebt = SB("ebt", [128, 4 * 128])
    att = SB("att", [128, 8 * TCH])
    ubuf = SB("ubuf", [128, 8 * TCH])
    u2b = SB("u2b", [128, TCH])
    statr = SB("statr", [1, 2 * TCH], F32)
    thin1 = SB("thin1", [1, TCH], F32)
    thin2 = SB("thin2", [1, TCH], F32)
    thin1b = SB("thin1b", [1, TCH])
    thin2b = SB("thin2b", [1, TCH])
    bc1s = SB("bc1s", [128, TCH])
    bc2s = SB("bc2s", [128, TCH])
    m2t = SB("m2t", [128, 1], F32)
    m2bb = SB("m2bb", [128, 1])
    wqf = SB("wqf", [128, 1024], F32)
    wqi = SB("wqi", [128, 1024], I32)
    wqb = SB("wqb", [128, 1024])
    wqtb = SB("wqtb", [128, 8 * 384])
    wdf = SB("wdf", [128, 1024], F32)
    wdi = SB("wdi", [128, 1024], I32)
    wdb = SB("wdb", [128, FFI])
    wdtb = SB("wdtb", [128, NIC * 128])
    wsc = SB("wsc", [128, 8], F32)
    scs8 = SB("scs8", [3, 384])
    scbuf = SB("scbuf", [1, 1536])
    r1t = SB("r1t", [128, 2 * TCH])
    qx = ubuf
    qxf = SB("qxf", [128, TCH], F32)
    qxi = SB("qxi", [128, TCH], I32)
    qgt2 = SB("qgt2", [128, 2 * 8 * 128])
    qut2 = SB("qut2", [128, 2 * 8 * 128])
    qdt2 = SB("qdt2", [128, 2 * NIC * 128])
    silu_t = SB("silu_t", [128, TCH], F32)
    h0t = SB("h0t", [128, 2 * TCH])
    h0q = qxf
    h0qi = qxi
    qh = dispF
    hmax = SB("hmax", [128, NIC], F32)
    hm1 = SB("hm1", [128, 1], F32)
    xmax8 = SB("xmax8", [128, 8], F32)
    hmx8 = SB("hmx8", [128, 8], F32)
    mxrow = statr
    thinsc = SB("thinsc", [1, 16], F32)
    sxb = SB("sxb", [128, 1], F32)
    sxs = SB("sxs", [128, 1], F32)
    shb = SB("shb", [128, 1], F32)
    shs = SB("shs", [128, 1], F32)
    swg_c = SB("swg_c", [128, NIC], F32)
    swu_c = SB("swu_c", [128, NIC], F32)
    swd_c = SB("swd_c", [128, 8], F32)
    sgcomb = SB("sgcomb", [128, NIC], F32)
    squcomb = SB("squcomb", [128, NIC], F32)
    sdcomb = SB("sdcomb", [128, 8], F32)
    ocast = silu_t
    otok = ebt
    o2t = dsl
    wdisp_bd = SB("wdisp_bd", [16, 2 * 128])
    sfwb = SB("sfwb", [32, 16])
    relb = SB("relb", [64, 16], F32)
    relT = SB("relT", [16, 64], F32)
    ppwb = SB("ppwb", [16, 16], F32)
    ppwT = SB("ppwT", [16, 16], F32)
    sfwT = SB("sfwT", [16, 32])
    posrow = SB("posrow", [16, 64], F32)
    sfbrow = SB("sfbrow", [1, 32], F32)
    sfbcol = SB("sfbcol", [32, 1], F32)
    sfbbc = SB("sfbbc", [128, 1], F32)
    wbrow = SB("wbrow", [1, 16])
    wdrow = SB("wdrow", [1, 16])
    wbcol = SB("wbcol", [16, 1])
    wdcol = SB("wdcol", [16, 1])
    wcol8 = SB("wcol8", [128, 8])
    wcol4a = SB("wcol4a", [128, 4], F32)

    psd = [PT(f"psd{i}", [128, TCH]) for i in range(4)]
    ps_e = PT("ps_e", [128, TCH])
    ps_f = PT("ps_f", [128, TCH])
    ps_big = PT("ps_big", [128, 1024])

    sem_names = [
        "dcast", "dzero", "dcons", "dsml", "vz", "dwvt", "dwct0", "dwct1",
        "dxa1a", "dxa1b", "dxua", "dxub", "h1a", "h1b", "dr1a", "dr1b2",
        "dgw0", "dgw1", "dgu0", "dgu1", "ddw0", "ddw1", "dq0", "dq1",
        "dh0a", "dh0b", "dh0ra", "dh0rb", "dwda", "dwdb",
        "h2s0", "h2s1", "h2s2", "h2s3",
        "suT", "suE", "suV",
        "dxa2", "a1T", "a1E", "valT", "valE",
        "bmT", "bmE", "m2V",
        "gsub", "gelu", "gfree", "dotr", "dotf", "sigS", "relT_", "e2S",
        "ecmb", "smmT", "rrV", "rbT", "rbE", "enV", "entT", "entE",
        "vwT", "vwE", "cpT", "uV", "sqS", "stT", "stE",
        "thV", "bcT", "bcE", "r1V", "dr1",
        "dwq", "wqV", "wqT", "wqE", "dagw", "agc", "dscl",
        "dmx1", "mxv", "scT", "scE", "scV",
        "dr1b", "qxV", "guT", "silS", "h0V", "hmV",
        "dmx2",
        "qhV", "dnT", "dnE",
        "o2V", "dob", "dotr2", "ocV", "dos",
        "dbgs",
    ]

    P = Prog()
    blk = st.enter_context(nc.Block())
    sems = {s: st.enter_context(nc.semaphore(s)) for s in sem_names}

    def dcol(pack, tl):
        return pack * (64 + SCW) + 64 + tl

    def fh(h):
        return h * FHSZ

    eb = lambda a: a.bitcast(BF16)

    with st:
        tk_xc0 = P.dma("gpsimd", lambda g: g.dma_start(
            out=AP(xbf, 0, [[C, T], [1, C]]), in_=x_in.ap()), inc="dcast")
        tk_wv = P.dma("gpsimd", lambda g: g.dma_start(
            out=AP(wvbf, 0, [[C, C], [1, C]]), in_=wval_in.ap()), inc="dcast")
        tk_wc = P.dma("gpsimd", lambda g: g.dma_start(
            out=AP(wcbf, 0, [[C, C], [1, C]]), in_=wcproj_in.ap()), inc="dcast")
        tk_xc = tk_wv = tk_wc = ("dcast", 48)

        tk_vz = P.add("vector", lambda v: v.memset(zerot[:], 0.0), inc="vz")
        P.add("vector", lambda v: v.memset(onescolb[:], 1.0), inc="vz")
        P.add("vector", lambda v: v.memset(onesrowb[:], 1.0), inc="vz")
        P.add("vector", lambda v: v.memset(sqbuf[:], 0.0), inc="vz")
        P.add("vector", lambda v: v.memset(xmax8[:], 0.0), inc="vz")
        P.add("vector", lambda v: v.memset(hmx8[:], 0.0), inc="vz")
        P.add("vector", lambda v: v.memset(hmax[:], 0.0), inc="vz")
        P.add("vector", lambda v: v.memset(dsl[:], 0.0), inc="vz")
        tk_vz_all = ("vz", P.cnt("vz"))

        FTOT = NH * FHSZ
        for off in range(0, FTOT, 128 * 512):
            n = min(128 * 512, FTOT - off)
            rows, rem = n // 512, n % 512
            if rows:
                P.dma("gpsimd", (lambda o, r: lambda g: g.dma_start(
                    out=AP(fbuf, o, [[512, r], [1, 512]]),
                    in_=zerot[0:r, :]))(off, rows), waits=[tk_vz], inc="dzero")
            if rem:
                P.dma("gpsimd", (lambda o, r: lambda g: g.dma_start(
                    out=AP(fbuf, o, [[r, 1], [1, r]]),
                    in_=zerot[0:1, 0:r]))(off + rows * 512, rem),
                    waits=[tk_vz], inc="dzero")
        tk_fz = ("dzero", P.cnt("dzero"))

        for dst, src in [(ident, identD), (identf, identFD), (prel, prelD),
                         (ind4, ind4D), (sel4, sel4D), (onesf, onesFD),
                         (cmask, cmaskD), (selbs, selbD), (rep32s, rep32D),
                         (rep16s, rep16D)]:
            P.dma("gpsimd", (lambda d, s: lambda g: g.dma_start(
                out=d[:], in_=s.ap()))(dst, src), inc="dcons")
        tk_cons = ("dcons", P.cnt("dcons"))

        for fn in [
            lambda g: g.dma_start(out=sfwb[:], in_=sfw_in.ap()),
            lambda g: g.dma_start(out=relb[:], in_=relpos_in.ap()),
            lambda g: g.dma_start(out=ppwb[:], in_=ppw_in.ap()),
            lambda g: g.dma_start(out=sfbrow[:], in_=AP(sfb_in, 0, [[32, 1], [1, 32]])),
            lambda g: g.dma_start(out=wbrow[:], in_=wbond_in.ap()),
            lambda g: g.dma_start(out=wdrow[:], in_=wdmg_in.ap()),
            lambda g: g.dma_start(out=thinsc[0:1, 0:1], in_=AP(dmgb_in, 0, [[1, 1], [1, 1]])),
            lambda g: g.dma_start(out=thinsc[0:1, 1:2], in_=AP(clipx_in, 0, [[1, 1], [1, 1]])),
            lambda g: g.dma_start(out=thinsc[0:1, 2:3], in_=AP(clipd_in, 0, [[1, 1], [1, 1]])),
        ]:
            P.dma("gpsimd", fn, inc="dsml")
        tk_sml = ("dsml", P.cnt("dsml"))

        for cc in range(8):
            for vv in range(8):
                P.dma("sync", (lambda cc, vv: lambda s: s.dma_start_transpose(
                    out=wvT[:, cc * 1024 + vv * 128: cc * 1024 + (vv + 1) * 128],
                    in_=AP(wvbf, vv * 128 * C + cc * 128, [[C, 128], [1, 128]])))(cc, vv),
                    waits=[tk_wv], inc="dwvt")
        tk_wvT = ("dwvt", P.cnt("dwvt"))

        P.mark('init')
        tk = P.add("tensor", lambda t: t.transpose(
            eb(ps_e[:])[0:16, 0:32], sfwb[0:32, 0:16], ident[0:32, 0:32]),
            waits=[tk_sml, tk_cons], inc="suT")
        tk_sfwT = P.add("scalar", lambda s: s.copy(sfwT[:], eb(ps_e[:])[0:16, 0:32]),
                        waits=[tk], inc="suE")
        tk = P.add("tensor", lambda t: t.transpose(ps_f[0:16, 0:64], relb[:], identf[0:64, 0:64]),
                   waits=[tk_sml, tk_cons, tk_sfwT], inc="suT")
        tk_relT = P.add("scalar", lambda s: s.copy(relT[:], ps_f[0:16, 0:64]),
                        waits=[tk], inc="suE")
        tk = P.add("tensor", lambda t: t.transpose(ps_f[0:16, 64:80], ppwb[:], identf[0:16, 0:16]),
                   waits=[tk_relT], inc="suT")
        tk_ppwT = P.add("scalar", lambda s: s.copy(ppwT[:], ps_f[0:16, 64:80]),
                        waits=[tk], inc="suE")
        tk = P.add("tensor", lambda t: t.matmul(ps_f[0:16, 0:64], ppwT[:], relT[:],
                                                start=True, stop=True),
                   waits=[tk_ppwT], inc="suT")
        tk_pos = P.add("scalar", lambda s: s.copy(posrow[:], ps_f[0:16, 0:64]),
                       waits=[tk], inc="suE")
        tk = P.add("tensor", lambda t: t.transpose(ps_f[0:32, 96:97], sfbrow[:], identf[0:1, 0:1]),
                   waits=[tk_pos], inc="suT")
        tk_sfbc = P.add("scalar", lambda s: s.copy(sfbcol[:], ps_f[0:32, 96:97]),
                        waits=[tk], inc="suE")
        tk = P.add("tensor", lambda t: t.matmul(ps_f[:, 100:101], rep32s[:], sfbcol[:],
                                                start=True, stop=True),
                   waits=[tk_sfbc], inc="suT")
        tk_sfbb = P.add("scalar", lambda s: s.copy(sfbbc[:], ps_f[:, 100:101]),
                        waits=[tk], inc="suE")
        tk = P.add("tensor", lambda t: t.matmul(ps_f[:, 0:64], rep16s[:], posrow[:],
                                                start=True, stop=True),
                   waits=[tk_sfbb], inc="suT")
        tk_ct = P.add("vector", lambda v: v.tensor_add(
            ctile[:], ps_f[:, 0:64], sfbbc[:].broadcast_to((128, 64))),
            waits=[tk], inc="suV")
        tk_cm = P.add("vector", lambda v: v.reduce_max(
            cmaxc[:], ctile[:], axis=AX.X, apply_absolute_value=True),
            waits=[tk_ct], inc="suV")
        tk_cm2 = P.add("vector", lambda v: v.tensor_scalar_add(cmaxc[:], cmaxc[:], 0.17),
                       waits=[tk_cm], inc="suV")
        tk = P.add("tensor", lambda t: t.matmul(ps_e[0:16, 0:1], wbrow[:],
                                                onescolb[0:1, 0:1], start=True, stop=True),
                   waits=[tk_sml, tk_vz_all, tk_sfwT], inc="suT")
        tk_wbc = P.add("scalar", lambda s: s.copy(wbcol[:], ps_e[0:16, 0:1]),
                       waits=[tk], inc="suE")
        tk = P.add("tensor", lambda t: t.matmul(ps_e[0:16, 1:2], wdrow[:],
                                                onescolb[0:1, 0:1], start=True, stop=True),
                   waits=[tk_wbc], inc="suT")
        tk_wdc = P.add("scalar", lambda s: s.copy(wdcol[:], ps_e[0:16, 1:2]),
                       waits=[tk], inc="suE")
        tk = None
        for k in range(8):
            col = wbcol if k < 4 else wdcol
            tk = P.add("tensor", (lambda k, col: lambda t: t.matmul(
                ps_e[:, 8 + k:9 + k], selbs[:, 128 * k:128 * (k + 1)], col[:],
                start=True, stop=True))(k, col), waits=[tk_wdc], inc="suT")
        tk_w8 = P.add("scalar", lambda s: s.copy(wcol8[:], ps_e[:, 8:16]),
                      waits=[tk], inc="suE")
        tk_w4a = P.add("scalar", lambda s: s.activation(
            wcol4a[:], ps_e[:, 8:12], AF.Abs), waits=[tk], inc="suE")
        tk_sq = P.add("scalar", lambda s: s.copy(
            AP(sqbuf, 15, [[160, 128], [16, 8]]),
            AP(wcol8, 0, [[8, 128], [1, 8]])),
            waits=[tk_w8, tk_vz_all], inc="suE")
        tk_sqa = P.add("scalar", lambda s: s.copy(
            AP(sqabs, 0, [[64, 128], [16, 4], [1, 16]]),
            AP(wcol4a, 0, [[4, 128], [1, 4], [0, 16]])),
            waits=[tk_w4a], inc="suE")
        tk = P.add("tensor", lambda t: t.matmul(ps_f[0:64, 102:103], onesf[0:1, 0:64],
                                                thinsc[0:1, 0:1], start=True, stop=True),
                   waits=[tk_sml, tk_sfbb], inc="suT")
        tk_dbc = P.add("scalar", lambda s: s.copy(dmgbc[:], ps_f[0:64, 102:103]),
                       waits=[tk], inc="suE")
        tk_wct2 = None
        wdctr = [0]
        for cc in range(8):
            tkm = None
            for h in range(NH):
                wsl = wdctr[0] % 2
                tk_wl = P.dma("gpsimd", (lambda cc, h, wsl: lambda g: g.dma_start(
                    out=wdisp_bd[:, wsl * 128:(wsl + 1) * 128],
                    in_=AP(wdisp_in, (h * 16) * C + cc * 128,
                           [[C, 16], [1, 128]])))(cc, h, wsl),
                    waits=[("suT", max(0, P.cnt("suT") - 1))],
                    inc="dwda" if wsl == 0 else "dwdb")
                wdctr[0] += 1
                tkm = P.add("tensor", (lambda cc, h, wsl: lambda t: t.matmul(
                    ps_f[:, 32 * h:32 * (h + 1)],
                    wdisp_bd[:, wsl * 128:(wsl + 1) * 128],
                    sfwT[:], start=True, stop=True))(cc, h, wsl),
                    waits=[tk_wl] + ([tk_sfwT, tk_dbc] if (h == 0 and cc == 0) else []) +
                          ([("suE", P.cnt("suE"))] if h == 0 else []),
                    inc="suT")
            tk_wct2 = P.add("scalar", (lambda cc: lambda s: s.copy(
                wcombT[:, cc * 512:(cc + 1) * 512], ps_f[:]))(cc),
                waits=[tkm], inc="suE")
        tk_setup = tk_wct2

        P.mark('setup')
        def wq_gateup(src_in, scrow_idx, waits0):
            tk_q = tk_T = None
            for kb in range(3):
                tk_l = P.dma("gpsimd", (lambda kb, src_in: lambda g: g.dma_start(
                    out=wqf[:],
                    in_=src_in.ap()[kb * 128:(kb + 1) * 128, :]))(kb, src_in),
                    waits=waits0 + [("wqV", P.cnt("wqV")), ("wqT", P.cnt("wqT"))],
                    inc="dwq")
                tk_m = P.add("vector", (lambda kb: lambda v: v.reduce_max(
                    wsc[:, kb:kb + 1], wqf[:], axis=AX.X,
                    apply_absolute_value=True))(kb),
                    waits=[tk_l], inc="wqV")
                t1 = P.add("vector", (lambda kb: lambda v: v.reciprocal(
                    wsc[:, 6:7], wsc[:, kb:kb + 1]))(kb), waits=[tk_m], inc="wqV")
                t2 = P.add("vector", lambda v: v.tensor_scalar_mul(
                    wsc[:, 6:7], wsc[:, 6:7], 127.0), waits=[t1], inc="wqV")
                t3 = P.add("vector", lambda v: v.tensor_scalar_mul(
                    wqf[:], wqf[:], wsc[:, 6:7]), waits=[t2], inc="wqV")
                t4 = P.add("vector", lambda v: v.tensor_copy(wqi[:], wqf[:]),
                           waits=[t3], inc="wqV")
                tk_q = P.add("vector", lambda v: v.tensor_copy(wqb[:], wqi[:]),
                             waits=[t4], inc="wqV")
                for hc in range(8):
                    tkt = P.add("tensor", (lambda hc: lambda t: t.transpose(
                        eb(ps_e[:])[:, 0:128],
                        wqb[:, hc * 128:(hc + 1) * 128], ident[:]))(hc),
                        waits=[tk_q, ("wqE", P.cnt("wqE"))], inc="wqT")
                    tk_T = P.add("scalar", (lambda hc, kb: lambda s: s.copy(
                        wqtb[:, hc * 384 + kb * 128: hc * 384 + (kb + 1) * 128],
                        eb(ps_e[:])[:, 0:128]))(hc, kb),
                        waits=[tkt], inc="wqE")
            tkt = P.add("tensor", lambda t: t.transpose(
                ps_f[0:3, 0:128], wsc[:, 0:3], identf[:]),
                waits=[tk_q, tk_setup], inc="wqT")
            tk_S = P.add("scalar", (lambda i0: lambda s: s.activation(
                scs8[0:3, i0 * 128 // 3 if False else (0 if i0 == 0 else 128):
                     (128 if i0 == 0 else 256)],
                ps_f[0:3, 0:128], AF.Copy,
                scale=1.0 / 127.0))(scrow_idx), waits=[tkt], inc="wqE")
            return tk_q, tk_T, tk_S

        tk_gq, tk_gT, tk_gS = wq_gateup(wg_in, 0, [tk_setup])
        tk_gW = P.dma("gpsimd", lambda g: g.dma_start(
            out=AP(agw_in, AG_QG, [[384, 128], [49152, 8], [1, 384]]),
            in_=AP(wqtb, 0, [[3072, 128], [384, 8], [1, 384]])),
            waits=[tk_gT], inc="dagw")
        tk_uq, tk_uT, tk_uS = wq_gateup(wu_in, 3, [tk_gW, tk_gS])
        tk_uW = P.dma("gpsimd", lambda g: g.dma_start(
            out=AP(agw_in, AG_QU, [[384, 128], [49152, 8], [1, 384]]),
            in_=AP(wqtb, 0, [[3072, 128], [384, 8], [1, 384]])),
            waits=[tk_uT], inc="dagw")
        tk_dm = None
        for pc in range(3):
            w_ = min(1024, FFI - pc * 1024)
            tk_dl = P.dma("gpsimd", (lambda pc, w_: lambda g: g.dma_start(
                out=wdf[:, 0:w_], in_=wd_in.ap()[:, pc * 1024:pc * 1024 + w_]))(pc, w_),
                waits=[tk_uq, ("wqV", P.cnt("wqV"))], inc="dwq")
            tk_r = P.add("vector", (lambda w_: lambda v: v.reduce_max(
                wsc[:, 5:6], wdf[:, 0:w_], axis=AX.X,
                apply_absolute_value=True))(w_), waits=[tk_dl], inc="wqV")
            if pc == 0:
                tk_dm = P.add("vector", lambda v: v.tensor_copy(
                    wsc[:, 6:7], wsc[:, 5:6]), waits=[tk_r], inc="wqV")
            else:
                tk_dm = P.add("vector", lambda v: v.tensor_max(
                    wsc[:, 6:7], wsc[:, 6:7], wsc[:, 5:6]), waits=[tk_r], inc="wqV")
        t1 = P.add("vector", lambda v: v.reciprocal(wsc[:, 7:8], wsc[:, 6:7]),
                   waits=[tk_dm], inc="wqV")
        t2 = P.add("vector", lambda v: v.tensor_scalar_mul(wsc[:, 7:8], wsc[:, 7:8], 127.0),
                   waits=[t1], inc="wqV")
        tk_dq = None
        for pc in range(3):
            w_ = min(1024, FFI - pc * 1024)
            tk_dl = P.dma("gpsimd", (lambda pc, w_: lambda g: g.dma_start(
                out=wdf[:, 0:w_], in_=wd_in.ap()[:, pc * 1024:pc * 1024 + w_]))(pc, w_),
                waits=[t2, ("wqV", P.cnt("wqV"))], inc="dwq")
            t3 = P.add("vector", (lambda w_: lambda v: v.tensor_scalar_mul(
                wdf[:, 0:w_], wdf[:, 0:w_], wsc[:, 7:8]))(w_),
                waits=[tk_dl], inc="wqV")
            t4 = P.add("vector", (lambda w_: lambda v: v.tensor_copy(
                wdi[:, 0:w_], wdf[:, 0:w_]))(w_), waits=[t3], inc="wqV")
            tk_dq = P.add("vector", (lambda pc, w_: lambda v: v.tensor_copy(
                wdb[:, pc * 1024:pc * 1024 + w_], wdi[:, 0:w_]))(pc, w_),
                waits=[t4], inc="wqV")
        tk_dT = None
        for grp in range(6):
            ntl = 4 if grp < 5 else 2
            tkt = None
            for j in range(ntl):
                ic = grp * 4 + j
                tkt = P.add("tensor", (lambda ic, j: lambda t: t.transpose(
                    eb(ps_e[:])[:, j * 128:(j + 1) * 128],
                    wdb[:, ic * 128:(ic + 1) * 128], ident[:]))(ic, j),
                    waits=[tk_dq] + ([("wqE", P.cnt("wqE"))] if j == 0 else []),
                    inc="wqT")
            tk_dT = P.add("scalar", (lambda grp, ntl: lambda s: s.copy(
                wdtb[:, grp * 512:grp * 512 + ntl * 128],
                eb(ps_e[:])[:, 0:ntl * 128]))(grp, ntl),
                waits=[tkt], inc="wqE")
        tk_dW = P.dma("gpsimd", lambda g: g.dma_start(
            out=AP(agw_in, AG_QD, [[128, 128], [16384, NIC], [1, 128]]),
            in_=AP(wdtb, 0, [[NIC * 128, 128], [128, NIC], [1, 128]])),
            waits=[tk_dT], inc="dagw")
        tkt = P.add("tensor", lambda t: t.transpose(ps_f[0:1, 0:128], wsc[:, 6:7],
                                                    identf[:]),
                    waits=[tk_dm, tk_uS], inc="wqT")
        tk_dS = P.add("scalar", lambda s: s.activation(
            scs8[0:1, 256:384], ps_f[0:1, 0:128], AF.Copy, scale=1.0 / 127.0),
            waits=[tkt], inc="wqE")
        P.dma("gpsimd", lambda g: g.dma_start(
            out=AP(ags_in, 0, [[128, 3], [1, 128]]), in_=scs8[0:3, 0:128]),
            waits=[tk_dS, tk_gS, tk_uS], inc="dscl")
        P.dma("gpsimd", lambda g: g.dma_start(
            out=AP(ags_in, 384, [[128, 3], [1, 128]]), in_=scs8[0:3, 128:256]),
            waits=[tk_dS, tk_gS, tk_uS], inc="dscl")
        tk_s8 = P.dma("gpsimd", lambda g: g.dma_start(
            out=AP(ags_in, 768, [[128, 1], [1, 128]]), in_=scs8[0:1, 256:384]),
            waits=[tk_dS, tk_gS, tk_uS], inc="dscl")
        tk_agw = P.add("gpsimd", lambda g: g.collective_compute(
            "AllGather", AluOpType.bypass, replica_groups=[list(range(NCORE))],
            ins=[agw_in.ap()], outs=[agw_out.ap()]),
            waits=[tk_gW, tk_uW, tk_dW], inc="agc")
        tk_ags = P.add("gpsimd", lambda g: g.collective_compute(
            "AllGather", AluOpType.bypass, replica_groups=[list(range(NCORE))],
            ins=[ags_in.ap()], outs=[ags_out.ap()]),
            waits=[tk_s8], inc="agc")
        tk_sclg = P.dma("gpsimd", lambda g: g.dma_start(
            out=scbuf[0:1, 0:1536],
            in_=AP(ags_out, 0, [[1024, 4], [128, 3], [1, 128]])),
            waits=[tk_ags], inc="dscl")

        P.mark('wq')
        tok_a1 = {}
        tok_val = {}
        tks_xu = {}
        tok_h1 = {}
        tok_h1slot = {}
        tok_r1 = {}
        tok_r1slot = {}
        a1T_after = {}

        for sc in range(NSC):
            if sc == 0:
                tk_halo = None
                for pack in range(4):
                    tk_halo = P.add("vector", (lambda pack: lambda v: v.memset(
                        dispF[:, pack * (64 + SCW): pack * (64 + SCW) + 64], 0.0))(pack),
                        inc="m2V")
            else:
                tk_halo = None
                for pack in range(4):
                    tk_halo = P.add("vector", (lambda pack: lambda v: v.tensor_copy(
                        dispF[:, pack * (64 + SCW): pack * (64 + SCW) + 64],
                        dispF[:, dcol(pack, SCW - 64): dcol(pack, SCW)]))(pack),
                        waits=[tok_a1[sc * 4 - 1], ("gsub", P.cnt("gsub"))],
                        inc="m2V")
            tk_halo_all = ("m2V", P.cnt("m2V"))

            for chl in range(4):
                ch = sc * 4 + chl
                tks_x = []
                for cc in range(8):
                    w0 = [tk_xc]
                    if cc >= 2 or ch > 0:
                        w0.append(("a1T", a1T_after.get((ch, cc - 2),
                                                        a1T_after.get((ch - 1, 7), 0))))
                    tks_x.append(P.dma("sync", (lambda ch, cc: lambda s:
                        s.dma_start_transpose(
                            out=xa1[:, (cc % 2) * TCH:((cc % 2) + 1) * TCH],
                            in_=AP(xbf, (ch * TCH) * C + cc * 128,
                                   [[C, TCH], [1, 128]])))(ch, cc),
                        waits=w0, inc="dxa1a" if cc % 2 == 0 else "dxa1b"))
                tkm = {}
                for cc in range(8):
                    for pack in range(4):
                        w0 = [tks_x[cc]]
                        if cc == 0 and pack == 0:
                            w0 += [tk_setup, ("dotf", P.cnt("dotf")),
                                   ("vwE", P.cnt("vwE")), ("a1E", P.cnt("a1E"))]
                        tkm[pack] = P.add("tensor",
                            (lambda cc, pack: lambda t: t.matmul(
                                psd[pack][:],
                                wcombT[:, cc * 512 + pack * 128: cc * 512 + (pack + 1) * 128],
                                xa1[:, (cc % 2) * TCH:((cc % 2) + 1) * TCH],
                                start=(cc == 0), stop=(cc == 7)))(cc, pack),
                            waits=w0, inc="a1T")
                    a1T_after[(ch, cc)] = P.cnt("a1T")
                tk_e = None
                for pack in range(4):
                    tk_e = P.add("scalar", (lambda chl, pack: lambda s: s.copy(
                        dispF[:, dcol(pack, chl * TCH): dcol(pack, (chl + 1) * TCH)],
                        psd[pack][:]))(chl, pack),
                        waits=[tkm[pack], tk_halo_all,
                               ("gsub", P.cnt("gsub")), ("m2V", P.cnt("m2V"))],
                        inc="a1E")
                tok_a1[ch] = tk_e

            P.mark(f'a1_sc{sc}')
            for pack in range(4):
                tk_m2 = P.add("vector", (lambda pack: lambda v: v.reduce_max(
                    m2t[:], dispF[:, pack * (64 + SCW): pack * (64 + SCW) + 64 + SCW],
                    axis=AX.X, apply_absolute_value=True))(pack),
                    waits=[tok_a1[sc * 4 + 3], tk_cm2], inc="m2V")
                tk_m2b = P.add("vector", lambda v: v.scalar_tensor_tensor(
                    m2t[:], m2t[:], 2.0, cmaxc[:], AluOpType.mult, AluOpType.add),
                    waits=[tk_m2], inc="m2V")
                tk_m2c = P.add("vector", lambda v: v.tensor_copy(m2bb[:], m2t[:]),
                               waits=[tk_m2b, ("bmT", P.cnt("bmT"))], inc="m2V")
                tkb = P.add("tensor", lambda t: t.matmul(
                    ps_e[0:64, 0:1], sqabs[:], m2bb[:], start=True, stop=True),
                    waits=[tk_m2c, tk_sqa, ("e2S", P.cnt("e2S")),
                           ("entE", P.cnt("entE")), ("rbE", P.cnt("rbE")),
                           ("bcE", P.cnt("bcE")), ("stE", P.cnt("stE"))],
                    inc="bmT")
                tk_negb = P.add("scalar", (lambda pack, sc: lambda s: s.activation(
                    negb[:, pack * NSC + sc: pack * NSC + sc + 1], ps_e[0:64, 0:1],
                    AF.Copy, scale=-1.02))(pack, sc),
                    waits=[tkb], inc="bmE")

                gctr = sc * 4 * 64 + pack * 64
                for wg in range(4):
                    for ws in range(16):
                        w = 16 * wg + ws
                        gi = gctr + w
                        gso = gi % 2
                        tk_sub = P.add("vector",
                            (lambda pack, w, gso: lambda v: v.tensor_sub(
                                gout[:, gso * SCW:(gso + 1) * SCW],
                                dispF[:, dcol(pack, 0) + w - 63:
                                      dcol(pack, 0) + w - 63 + SCW],
                                dispF[:, dcol(pack, 0): dcol(pack, SCW)]))(pack, w, gso),
                            waits=[tok_a1[sc * 4 + 3],
                                   ("gfree", max(0, P.cnt("gfree") - 1))],
                            inc="gsub")
                        tk_gel = P.add("scalar",
                            (lambda w, gso: lambda s: s.activation(
                                gout[:, gso * SCW:(gso + 1) * SCW],
                                gout[:, gso * SCW:(gso + 1) * SCW],
                                AF.Gelu, bias=ctile[:, w:w + 1]))(w, gso),
                            waits=[tk_sub],
                            inc="gelu")
                        for sub in range(4):
                            wdep = [tk_gel] if sub == 0 else []
                            if ws == 0 and sub == 0:
                                wdep.append(("dotf", P.cnt("dotf")))
                            P.add("tensor",
                                (lambda ws, sub, gso: lambda t: t.matmul(
                                    psd[sub][:], sqbuf[:, 15 - ws: 143 - ws],
                                    gout[:, gso * SCW + sub * TCH:
                                         gso * SCW + (sub + 1) * TCH],
                                    start=(ws == 0), stop=(ws == 15)))(ws, sub, gso),
                                waits=wdep,
                                inc="dotr" if ws == 15 else None,
                                inc2="gfree" if sub == 3 else None)
                    for sub in range(4):
                        dslp = sub % 2
                        tk_sg = P.add("scalar",
                            (lambda sub, dslp: lambda s: s.activation(
                                dsl[64:128, dslp * TCH:(dslp + 1) * TCH],
                                psd[sub][64:128, :], AF.Sigmoid, bias=dmgbc[:]))(sub, dslp),
                            waits=[("dotr", P.cnt("dotr") - (3 - sub)), tk_dbc,
                                   ("relT_", max(0, P.cnt("relT_") - 1))],
                            inc="sigS")
                        tk_e1 = P.add("scalar",
                            (lambda sub, pack, sc: lambda s: s.activation(
                                e1t[:], psd[sub][0:64, :], AF.Exp,
                                bias=negb[:, pack * NSC + sc: pack * NSC + sc + 1]))(
                                sub, pack, sc),
                            waits=[tk_negb, ("ecmb", P.cnt("ecmb"))],
                            inc="sigS", inc2="dotf")
                        tk_rl = P.add("tensor",
                            (lambda dslp: lambda t: t.matmul(
                                ps_e[0:64, :], prel[:],
                                dsl[:, dslp * TCH:(dslp + 1) * TCH],
                                start=True, stop=True))(dslp),
                            waits=[tk_sg, ("e2S", P.cnt("e2S")),
                                   ("bmE", P.cnt("bmE")),
                                   ("entE", P.cnt("entE")),
                                   ("rbE", P.cnt("rbE")),
                                   ("rrV", P.cnt("rrV"))],
                            inc="relT_")
                        tk_e2 = P.add("scalar", lambda s: s.activation(
                            e2t[:], ps_e[0:64, :], AF.Exp, scale=-10.0),
                            waits=[tk_rl], inc="e2S")
                        tk_cb = P.add("vector",
                            (lambda wg, sub: lambda v: v.tensor_mul(
                                etile[:, (wg * 4 + sub) * TCH:(wg * 4 + sub + 1) * TCH],
                                e1t[:], e2t[:]))(wg, sub),
                            waits=[tk_e1, tk_e2, ("entT", P.cnt("entT")),
                                   ("smmT", P.cnt("smmT"))],
                            inc="ecmb")
                        if sc == 0 and sub == 0:
                            P.add("vector", (lambda wg: lambda v: v.tensor_mul(
                                etile[:, (wg * 4) * TCH:(wg * 4 + 1) * TCH],
                                etile[:, (wg * 4) * TCH:(wg * 4 + 1) * TCH],
                                cmask[:, wg * TCH:(wg + 1) * TCH]))(wg),
                                waits=[tk_cb, tk_cons], inc="ecmb")
                P.mark(f'win_sc{sc}_p{pack}')
                tok_ecmb_all = ("ecmb", P.cnt("ecmb"))
                for sub in range(4):
                    tks = None
                    for wg in range(4):
                        tks = P.add("tensor", (lambda wg, sub: lambda t: t.matmul(
                            ps_e[0:4, :], ind4[:],
                            etile[:, (wg * 4 + sub) * TCH:(wg * 4 + sub + 1) * TCH],
                            start=(wg == 0), stop=(wg == 3)))(wg, sub),
                            waits=[tok_ecmb_all] +
                                  ([("rrV", P.cnt("rrV")),
                                    ("rbE", P.cnt("rbE")),
                                    ("e2S", P.cnt("e2S"))] if wg == 0 else []),
                            inc="smmT")
                    tk_rr = P.add("vector", lambda v: v.reciprocal(rrow[:], ps_e[0:4, :]),
                                  waits=[tks], inc="rrV")
                    tk_rrb = P.add("vector", lambda v: v.tensor_copy(rrowb[:], rrow[:]),
                                   waits=[tk_rr], inc="rrV")
                    tk_rb = P.add("tensor", lambda t: t.matmul(
                        ps_e[0:64, :], sel4[:], rrowb[:], start=True, stop=True),
                        waits=[tk_rrb], inc="rbT")
                    tk_rbe = P.add("scalar", lambda s: s.copy(rbev[:], ps_e[0:64, :]),
                                   waits=[tk_rb], inc="rbE")
                    for wg in range(4):
                        P.add("vector", (lambda wg, sub: lambda v: v.tensor_mul(
                            etile[:, (wg * 4 + sub) * TCH:(wg * 4 + sub + 1) * TCH],
                            etile[:, (wg * 4 + sub) * TCH:(wg * 4 + sub + 1) * TCH],
                            rbev[:]))(wg, sub),
                            waits=[tk_rbe], inc="enV")
                tok_en_all = ("enV", P.cnt("enV"))
                for qtl in range(16):
                    qt = sc * 16 + qtl
                    sub = qtl // 4
                    eslot = (qt * 4 + pack) % 2
                    tkt = None
                    for wg in range(4):
                        tkt = P.add("tensor", (lambda wg, sub, qtl: lambda t: t.transpose(
                            eb(ps_e[:])[:, wg * 64:(wg + 1) * 64],
                            etile[:, (wg * 4 + sub) * TCH + (qtl % 4) * 128:
                                  (wg * 4 + sub) * TCH + (qtl % 4 + 1) * 128],
                            ident[0:64, 0:64]))(wg, sub, qtl),
                            waits=[tok_en_all] +
                                  ([("rbE", P.cnt("rbE")),
                                    ("entE", P.cnt("entE"))] if wg == 0 else []),
                            inc="entT")
                    tk_ee = P.add("scalar", (lambda eslot: lambda s: s.copy(
                        AP(ent, eslot * 256, [[512, 128], [64, 4], [16, 4], [1, 16]]),
                        AP(ps_e.bitcast(BF16), 0,
                           [[1024, 128], [16, 4], [64, 4], [1, 16]])))(eslot),
                        waits=[tkt, tok_h1slot.get(eslot)], inc="entE")
                    tok_h1[(qt, pack)] = P.dma("gpsimd",
                        (lambda qt, pack, eslot: lambda g: g.dma_start(
                            out=AP(fbuf, fh(pack * 4) + FOFF + (qt * 128) * FROW,
                                   [[FROW, 128], [FHSZ, 4], [1, 64]]),
                            in_=AP(ent, eslot * 256, [[512, 128], [64, 4], [1, 64]])))(
                            qt, pack, eslot),
                        waits=[tk_ee, tk_fz], inc="h1a" if eslot == 0 else "h1b")
                    tok_h1slot[eslot] = tok_h1[(qt, pack)]

            P.mark(f'packs_sc{sc}')
            for chl in range(4):
                ch = sc * 4 + chl
                for qs in range(4):
                    qt = ch * 4 + qs
                    tkv = None
                    for wave in range(2):
                        tks_x2 = []
                        for j in range(4):
                            cc = wave * 4 + j
                            tks_x2.append(P.dma("sync", (lambda ch, cc, j: lambda s:
                                s.dma_start_transpose(
                                    out=xa2[:, j * TCH:(j + 1) * TCH],
                                    in_=AP(xbf, (ch * TCH) * C + cc * 128,
                                           [[C, TCH], [1, 128]])))(ch, cc, j),
                                waits=[tk_xc, ("valT", P.cnt("valT"))], inc="dxa2"))
                        tok_xa2 = ("dxa2", P.cnt("dxa2"))
                        for j in range(4):
                            cc = wave * 4 + j
                            for half in range(2):
                                w0 = []
                                if j == 0 and half == 0:
                                    w0 = [tok_xa2, tk_wvT]
                                    if wave == 0:
                                        w0 += [("valE", P.cnt("valE")),
                                               ("vwT", P.cnt("vwT"))]
                                tkv = P.add("tensor",
                                    (lambda qs, cc, j, half: lambda t: t.matmul(
                                        ps_big[:, half * 512:(half + 1) * 512],
                                        xa2[:, j * TCH + qs * 128:
                                            j * TCH + (qs + 1) * 128],
                                        wvT[:, cc * 1024 + half * 512:
                                            cc * 1024 + (half + 1) * 512],
                                        start=(cc == 0), stop=(cc == 7)))(qs, cc, j, half),
                                    waits=w0, inc="valT")
                    tok_val[qt] = P.add("scalar", (lambda qt: lambda s: s.copy(
                        valr[:, (qt % 5) * 1024:((qt % 5) + 1) * 1024], ps_big[:]))(qt),
                        waits=[tkv], inc="valE")
                for qs in range(4):
                    qt = ch * 4 + qs
                    for h in range(NH):
                        pack = h // 4
                        ebs0 = (2 * (qt * NH + h)) % 4
                        ebs1 = (2 * (qt * NH + h) + 1) % 4
                        deps = [tok_h1[(qt, pack)]]
                        if qt > 0:
                            deps.append(tok_h1[(qt - 1, pack)])
                        tk_b1 = None
                        if qt > 0:
                            tk_b1 = P.dma("sync", (lambda qt, h, ebs0: lambda s:
                                s.dma_start_transpose(
                                    out=ebt[:, ebs0 * 128:(ebs0 + 1) * 128],
                                    in_=AP(fbuf, fh(h) + FOFF + 256 * (qt * 128) - 65,
                                           [[255, 128], [1, 128]])))(qt, h, ebs0),
                                waits=deps + [("vwT", max(0, P.cnt("vwT") - 4))],
                                inc=f"h2s{ebs0}")
                        tk_b2 = P.dma("sync", (lambda qt, h, ebs1: lambda s:
                            s.dma_start_transpose(
                                out=ebt[:, ebs1 * 128:(ebs1 + 1) * 128],
                                in_=AP(fbuf, fh(h) + FOFF + 256 * (qt * 128) + 63,
                                       [[255, 128], [1, 128]])))(qt, h, ebs1),
                            waits=deps + [("vwT", max(0, P.cnt("vwT") - 4))],
                            inc=f"h2s{ebs1}")
                        first = True
                        tkw = None
                        if qt > 0:
                            tkw = P.add("tensor",
                                (lambda qt, h, qs, ebs0: lambda t: t.matmul(
                                    psd[0][64 * (h % 2):64 * (h % 2) + 64,
                                           qs * 128:(qs + 1) * 128],
                                    valr[:, ((qt - 1) % 5) * 1024 + h * 64:
                                         ((qt - 1) % 5) * 1024 + (h + 1) * 64],
                                    ebt[:, ebs0 * 128:(ebs0 + 1) * 128],
                                    start=True, stop=False,
                                    tile_position=(0, 64 * (h % 2))))(qt, h, qs, ebs0),
                                waits=[tk_b1, tok_val[max(0, qt - 1)], tok_val[qt],
                                       ("vwE", P.cnt("vwE"))],
                                inc="vwT")
                            first = False
                        tkw = P.add("tensor",
                            (lambda qt, h, qs, ebs1, first: lambda t: t.matmul(
                                psd[0][64 * (h % 2):64 * (h % 2) + 64,
                                       qs * 128:(qs + 1) * 128],
                                valr[:, (qt % 5) * 1024 + h * 64:
                                     (qt % 5) * 1024 + (h + 1) * 64],
                                ebt[:, ebs1 * 128:(ebs1 + 1) * 128],
                                start=first, stop=True,
                                tile_position=(0, 64 * (h % 2))))(qt, h, qs, ebs1, first),
                            waits=[tk_b2, tok_val[qt]] +
                                  ([("vwE", P.cnt("vwE"))] if first else []),
                            inc="vwT")
                        if h % 2 == 1:
                            P.add("scalar",
                                (lambda h, qs: lambda s: s.copy(
                                    att[:, (h // 2) * TCH + qs * 128:
                                        (h // 2) * TCH + (qs + 1) * 128],
                                    psd[0][:, qs * 128:(qs + 1) * 128]))(h, qs),
                                waits=[tkw, ("cpT", P.cnt("cpT"))], inc="vwE")
                tok_vw = ("vwE", P.cnt("vwE"))
                tok_u = {}
                for cc in range(8):
                    tks_xu[cc % 2] = P.dma("sync", (lambda ch, cc: lambda s:
                        s.dma_start_transpose(
                            out=xur[:, (cc % 2) * TCH:((cc % 2) + 1) * TCH],
                            in_=AP(xbf, (ch * TCH) * C + cc * 128,
                                   [[C, TCH], [1, 128]])))(ch, cc),
                        waits=[tk_xc, tok_u.get(cc - 2)],
                        inc="dxua" if cc % 2 == 0 else "dxub")
                    wslot = cc % 2
                    tk_wl = None
                    for vc in range(8):
                        tk_wl = P.dma("sync", (lambda cc, vc, wslot: lambda s:
                            s.dma_start_transpose(
                                out=wcT2[:, wslot * 1024 + vc * 128:
                                     wslot * 1024 + (vc + 1) * 128],
                                in_=AP(wcbf, cc * 128 * C + vc * 128,
                                       [[C, 128], [1, 128]])))(cc, vc, wslot),
                            waits=[tk_wc, ("cpT", max(0, P.cnt("cpT") - 8))],
                            inc="dwct0" if wslot == 0 else "dwct1")
                    tkc = None
                    for vc in range(8):
                        w0 = [tk_wl] if vc == 0 else []
                        if vc == 0:
                            w0 += [tok_vw, ("uV", P.cnt("uV")),
                                   ("a1E", P.cnt("a1E"))]
                        tkc = P.add("tensor", (lambda cc, vc, wslot: lambda t: t.matmul(
                            ps_f[:],
                            wcT2[:, wslot * 1024 + vc * 128:
                                 wslot * 1024 + (vc + 1) * 128],
                            att[:, vc * TCH:(vc + 1) * TCH],
                            start=(vc == 0), stop=(vc == 7)))(cc, vc, wslot),
                            waits=w0, inc="cpT")
                    tok_u[cc] = P.add("vector", (lambda cc: lambda v: v.tensor_add(
                        ubuf[:, cc * TCH:(cc + 1) * TCH],
                        xur[:, (cc % 2) * TCH:((cc % 2) + 1) * TCH],
                        ps_f[:]))(cc),
                        waits=[tkc, tks_xu[cc % 2], ("stT", P.cnt("stT")),
                               ("sqS", P.cnt("sqS"))], inc="uV")
                tok_u_all = ("uV", P.cnt("uV"))
                tkst = None
                for cc in range(8):
                    w0 = [tok_u_all]
                    if cc == 0:
                        w0 += [("stE", P.cnt("stE")), ("entE", P.cnt("entE")),
                               ("rrV", P.cnt("rrV")), ("e2S", P.cnt("e2S")),
                               ("bcE", P.cnt("bcE"))]
                    tkst = P.add("tensor", (lambda cc: lambda t: t.matmul(
                        ps_e[0:1, 0:TCH], onescolb[:],
                        ubuf[:, cc * TCH:(cc + 1) * TCH],
                        start=(cc == 0), stop=(cc == 7)))(cc),
                        waits=w0, inc="stT")
                tk_se1 = P.add("scalar", lambda s: s.copy(
                    statr[0:1, 0:TCH], ps_e[0:1, 0:TCH]), waits=[tkst], inc="stE")
                tkst = None
                for cc in range(8):
                    tksq = P.add("scalar", (lambda cc: lambda s: s.activation(
                        u2b[:], ubuf[:, cc * TCH:(cc + 1) * TCH], AF.Square))(cc),
                        waits=[tk_se1] if cc == 0 else [("stT", P.cnt("stT"))],
                        inc="sqS")
                    tkst = P.add("tensor", (lambda cc: lambda t: t.matmul(
                        ps_e[0:1, 0:TCH], onescolb[:], u2b[:],
                        start=(cc == 0), stop=(cc == 7)))(cc),
                        waits=[tksq], inc="stT")
                tk_se2 = P.add("scalar", lambda s: s.copy(
                    statr[0:1, TCH:2 * TCH], ps_e[0:1, 0:TCH]), waits=[tkst], inc="stE")
                tk1 = P.add("vector", lambda v: v.tensor_scalar_mul(
                    thin2[0:1, :], statr[0:1, 0:TCH], 1.0 / C), waits=[tk_se1], inc="thV")
                tk2 = P.add("vector", lambda v: v.tensor_scalar_mul(
                    thin1[0:1, :], statr[0:1, TCH:2 * TCH], 1.0 / C),
                    waits=[tk_se2, tk1], inc="thV")
                tk3 = P.add("vector", lambda v: v.tensor_mul(
                    u2b[0:1, 0:TCH], thin2[0:1, :], thin2[0:1, :]), waits=[tk2], inc="thV")
                tk4 = P.add("vector", lambda v: v.tensor_sub(
                    thin1[0:1, :], thin1[0:1, :], u2b[0:1, 0:TCH]), waits=[tk3], inc="thV")
                tk5 = P.add("vector", lambda v: v.tensor_scalar_add(
                    thin1[0:1, :], thin1[0:1, :], 1e-5), waits=[tk4], inc="thV")
                tk6 = P.add("scalar", lambda s: s.activation(
                    thin1[0:1, :], thin1[0:1, :], AF.Sqrt), waits=[tk5], inc="sqS")
                tk7 = P.add("vector", lambda v: v.reciprocal(
                    thin1[0:1, :], thin1[0:1, :]), waits=[tk6], inc="thV")
                tk8 = P.add("vector", lambda v: v.tensor_mul(
                    thin2[0:1, :], thin2[0:1, :], thin1[0:1, :]), waits=[tk7], inc="thV")
                tk9 = P.add("vector", lambda v: v.tensor_copy(
                    thin1b[0:1, :], thin1[0:1, :]), waits=[tk8], inc="thV")
                tk10 = P.add("vector", lambda v: v.tensor_copy(
                    thin2b[0:1, :], thin2[0:1, :]), waits=[tk9], inc="thV")
                tb1 = P.add("tensor", lambda t: t.matmul(
                    ps_e[:], onesrowb[:], thin1b[0:1, :], start=True, stop=True),
                    waits=[tk9, ("bcE", P.cnt("bcE")), ("stE", P.cnt("stE"))], inc="bcT")
                tb1e = P.add("scalar", lambda s: s.copy(bc1s[:], ps_e[:]),
                             waits=[tb1], inc="bcE")
                tb2 = P.add("tensor", lambda t: t.matmul(
                    ps_e[:], onesrowb[:], thin2b[0:1, :], start=True, stop=True),
                    waits=[tk10, tb1e], inc="bcT")
                tb2e = P.add("scalar", lambda s: s.copy(bc2s[:], ps_e[:]),
                             waits=[tb2], inc="bcE")
                for cc in range(8):
                    tkr = P.add("vector", (lambda cc: lambda v: v.tensor_mul(
                        r1t[:, (cc % 2) * TCH:((cc % 2) + 1) * TCH],
                        ubuf[:, cc * TCH:(cc + 1) * TCH], bc1s[:]))(cc),
                        waits=[tb1e, tok_r1slot.get(cc % 2)], inc="r1V")
                    tkr2 = P.add("vector", (lambda cc: lambda v: v.tensor_sub(
                        r1t[:, (cc % 2) * TCH:((cc % 2) + 1) * TCH],
                        r1t[:, (cc % 2) * TCH:((cc % 2) + 1) * TCH], bc2s[:]))(cc),
                        waits=[tb2e, tkr], inc="r1V")
                    tok_r1[(ch, cc)] = P.dma("gpsimd",
                        (lambda ch, cc: lambda g: g.dma_start(
                            out=AP(r1d, cc * 128 * T + ch * TCH, [[T, 128], [1, TCH]]),
                            in_=r1t[:, (cc % 2) * TCH:((cc % 2) + 1) * TCH]))(ch, cc),
                        waits=[tkr2], inc="dr1a" if cc % 2 == 0 else "dr1b2")
                    tok_r1slot[cc % 2] = tok_r1[(ch, cc)]
                    tkxm = P.add("vector", (lambda cc: lambda v: v.reduce_max(
                        xmax8[:, 1:2], r1t[:, (cc % 2) * TCH:((cc % 2) + 1) * TCH],
                        axis=AX.X, apply_absolute_value=True))(cc),
                        waits=[tkr2], inc="r1V")
                    P.add("vector", lambda v: v.tensor_max(
                        xmax8[:, 0:1], xmax8[:, 0:1], xmax8[:, 1:2]),
                        waits=[tkxm], inc="r1V")
                P.mark(f'ch{ch}')

        P.mark('attn')
        tok_r1_all = [("dr1a", P.cnt("dr1a")), ("dr1b2", P.cnt("dr1b2"))]
        tok_r1v_all = ("r1V", P.cnt("r1V"))

        P.add("vector", lambda v: v.memset(xmax8[:, 1:8], 0.0),
              waits=[tok_r1v_all], inc="r1V")
        tk_xm = P.dma("gpsimd", lambda g: g.dma_start(
            out=AP(mx1_in, 0, [[8, 128], [1, 8]]), in_=xmax8[:]),
            waits=[("r1V", P.cnt("r1V"))], inc="dmx1")
        tk_ar1 = P.add("gpsimd", lambda g: g.collective_compute(
            "AllReduce", AluOpType.max, replica_groups=[list(range(NCORE))],
            ins=[mx1_in.ap()], outs=[mx1_out.ap()]),
            waits=[tk_xm, ("agc", 2)], inc="agc")
        tk_mxl = P.dma("gpsimd", lambda g: g.dma_start(
            out=mxrow[0:1, 0:1024], in_=AP(mx1_out, 0, [[1024, 1], [1, 1024]])),
            waits=[tk_ar1], inc="dmx1")
        tk_gx = P.add("vector", lambda v: v.reduce_max(
            thinsc[0:1, 3:4], mxrow[0:1, 0:1024], axis=AX.X), waits=[tk_mxl], inc="mxv")
        tk_sg1 = P.add("scalar", lambda s: s.activation(
            thinsc[0:1, 4:5], thinsc[0:1, 1:2], AF.Sigmoid),
            waits=[tk_sml, tk_gx], inc="scE")
        tk_sx1 = P.add("vector", lambda v: v.tensor_mul(
            thinsc[0:1, 5:6], thinsc[0:1, 3:4], thinsc[0:1, 4:5]),
            waits=[tk_sg1], inc="mxv")
        tk_sx2 = P.add("vector", lambda v: v.tensor_scalar_mul(
            thinsc[0:1, 5:6], thinsc[0:1, 5:6], 1.0 / 127.0), waits=[tk_sx1], inc="mxv")
        tk_sx3 = P.add("vector", lambda v: v.reciprocal(
            thinsc[0:1, 6:7], thinsc[0:1, 5:6]), waits=[tk_sx2], inc="mxv")
        tkb = P.add("tensor", lambda t: t.matmul(
            ps_e[:, 0:1], onesf[:], thinsc[0:1, 6:7], start=True, stop=True),
            waits=[tk_sx3, ("bcE", P.cnt("bcE"))], inc="scT")
        tk_sxb = P.add("scalar", lambda s: s.copy(sxb[:], ps_e[:, 0:1]),
                       waits=[tkb], inc="scE")
        tkb = P.add("tensor", lambda t: t.matmul(
            ps_e[:, 1:2], onesf[:], thinsc[0:1, 5:6], start=True, stop=True),
            waits=[tk_sx3, tk_sxb], inc="scT")
        tk_sxs = P.add("scalar", lambda s: s.copy(sxs[:], ps_e[:, 1:2]),
                       waits=[tkb], inc="scE")
        tkb = None
        tk_prev = tk_sclg
        for grp in range(2):
            n_ic = 12 if grp == 0 else 10
            if grp == 1:
                tk_prev = P.dma("gpsimd", lambda g: g.dma_start(
                    out=scbuf[0:1, 0:1536],
                    in_=AP(ags_out, 4 * 1024, [[1024, 4], [128, 3], [1, 128]])),
                    waits=[tk_ags, ("scT", P.cnt("scT"))], inc="dscl")
            for j in range(n_ic):
                ic = grp * 12 + j
                tkb = P.add("tensor", (lambda ic, j: lambda t: t.matmul(
                    ps_e[:, 2 + ic:3 + ic],
                    scbuf[0:1, j * 128:(j + 1) * 128],
                    onescolb[0:1, 0:1], start=True, stop=True))(ic, j),
                    waits=[tk_prev, tk_sxs] if j == 0 else [], inc="scT")
        tk_swg = P.add("scalar", lambda s: s.copy(swg_c[:], ps_e[:, 2:2 + NIC]),
                       waits=[tkb], inc="scE")
        tkb = None
        tk_prev = None
        for grp in range(2):
            n_ic = 12 if grp == 0 else 10
            off = 384 if grp == 0 else 4 * 1024 + 384
            ncr = 4
            tk_prev = P.dma("gpsimd", (lambda off: lambda g: g.dma_start(
                out=scbuf[0:1, 0:1536],
                in_=AP(ags_out, off, [[1024, 4], [128, 3], [1, 128]])))(off),
                waits=[tk_ags, ("scT", P.cnt("scT"))], inc="dscl")
            for j in range(n_ic):
                ic = grp * 12 + j
                tkb = P.add("tensor", (lambda ic, j: lambda t: t.matmul(
                    ps_e[:, 2 + ic:3 + ic],
                    scbuf[0:1, j * 128:(j + 1) * 128],
                    onescolb[0:1, 0:1], start=True, stop=True))(ic, j),
                    waits=[tk_prev, tk_swg] if j == 0 else [], inc="scT")
        tk_swu = P.add("scalar", lambda s: s.copy(swu_c[:], ps_e[:, 2:2 + NIC]),
                       waits=[tkb], inc="scE")
        tk_scld = P.dma("gpsimd", lambda g: g.dma_start(
            out=scbuf[0:1, 0:1024],
            in_=AP(ags_out, 768, [[1024, 8], [1, 128]])),
            waits=[tk_ags, ("scT", P.cnt("scT"))], inc="dscl")
        tkb = None
        for hc in range(8):
            tkb = P.add("tensor", (lambda hc: lambda t: t.matmul(
                ps_e[:, 2 + hc:3 + hc],
                scbuf[0:1, hc * 128:(hc + 1) * 128],
                onescolb[0:1, 0:1], start=True, stop=True))(hc),
                waits=[tk_scld, tk_swu] if hc == 0 else [], inc="scT")
        tk_swd = P.add("scalar", lambda s: s.copy(swd_c[:], ps_e[:, 2:10]),
                       waits=[tkb], inc="scE")
        tk_sgc = P.add("vector", lambda v: v.tensor_mul(
            sgcomb[:], swg_c[:], sxs[:].broadcast_to((128, NIC))),
            waits=[tk_swg, tk_sxs], inc="scV")

        P.mark('ffnsc')
        tok_h0 = {}
        tok_h0slot = {}
        for ch in range(NCH):
            tok_qx = {}
            for cc in range(8):
                tl = P.dma("gpsimd", (lambda ch, cc: lambda g: g.dma_start(
                    out=r1t[:, (cc % 2) * TCH:((cc % 2) + 1) * TCH],
                    in_=AP(r1d, cc * 128 * T + ch * TCH, [[T, 128], [1, TCH]])))(ch, cc),
                    waits=tok_r1_all + [("qxV", max(0, P.cnt("qxV") - 4))],
                    inc="dq0" if cc % 2 == 0 else "dq1")
                tq1 = P.add("vector", (lambda cc: lambda v: v.tensor_scalar_mul(
                    qxf[:], r1t[:, (cc % 2) * TCH:((cc % 2) + 1) * TCH], sxb[:]))(cc),
                    waits=[tl, tk_sxb, ("guT", P.cnt("guT"))], inc="qxV")
                tq2 = P.add("vector", lambda v: v.tensor_copy(qxi[:], qxf[:]),
                            waits=[tq1], inc="qxV")
                tq3 = P.add("vector", lambda v: v.tensor_copy(qxf[:], qxi[:]),
                            waits=[tq2], inc="qxV")
                tok_qx[cc] = P.add("vector", (lambda cc: lambda v: v.tensor_scalar(
                    qx[:, cc * TCH:(cc + 1) * TCH], qxf[:], 127.0, -128.0,
                    AluOpType.min, AluOpType.max))(cc),
                    waits=[tq3], inc="qxV")
            tok_qx_all = ("qxV", P.cnt("qxV"))
            for ic in range(NIC):
                gslot = ic % 2
                r, loc = ic // 3, ic % 3
                tgw = P.dma("gpsimd", (lambda r, loc, gslot: lambda g: g.dma_start(
                    out=qgt2[:, gslot * 1024:(gslot + 1) * 1024],
                    in_=AP(agw_out, r * AG_SZ + AG_QG + loc * 128,
                           [[384, 128], [128 * 384, 8], [1, 128]])))(r, loc, gslot),
                    waits=[("agc", 1), ("guT", max(0, P.cnt("guT") - 17))],
                    inc="dgw0" if gslot == 0 else "dgw1")
                tuw = P.dma("gpsimd", (lambda r, loc, gslot: lambda g: g.dma_start(
                    out=qut2[:, gslot * 1024:(gslot + 1) * 1024],
                    in_=AP(agw_out, r * AG_SZ + AG_QU + loc * 128,
                           [[384, 128], [128 * 384, 8], [1, 128]])))(r, loc, gslot),
                    waits=[("agc", 1), ("guT", max(0, P.cnt("guT") - 17))],
                    inc="dgu0" if gslot == 0 else "dgu1")
                tkg = None
                for hc in range(8):
                    w0 = [tgw, tok_qx_all] if hc == 0 else []
                    if hc == 0:
                        w0.append(("silS", P.cnt("silS")))
                        w0.append(("h0V", P.cnt("h0V")))
                    tkg = P.add("tensor", (lambda hc, gslot: lambda t: t.matmul(
                        ps_big[:, 0:512],
                        qgt2[:, gslot * 1024 + hc * 128: gslot * 1024 + (hc + 1) * 128],
                        qx[:, hc * TCH:(hc + 1) * TCH],
                        start=(hc == 0), stop=(hc == 7)))(hc, gslot),
                        waits=w0, inc="guT")
                tku = None
                for hc in range(8):
                    w0 = [tuw] if hc == 0 else []
                    tku = P.add("tensor", (lambda hc, gslot: lambda t: t.matmul(
                        ps_big[:, 512:1024],
                        qut2[:, gslot * 1024 + hc * 128: gslot * 1024 + (hc + 1) * 128],
                        qx[:, hc * TCH:(hc + 1) * TCH],
                        start=(hc == 0), stop=(hc == 7)))(hc, gslot),
                        waits=w0, inc="guT")
                tk_sil = P.add("scalar", (lambda ic: lambda s: s.activation(
                    silu_t[:], ps_big[:, 0:512], AF.Silu,
                    scale=sgcomb[:, ic:ic + 1]))(ic),
                    waits=[tkg, tk_sgc, ("h0V", P.cnt("h0V"))], inc="silS")
                hslot = ic % 2
                tk_h0 = P.add("vector", (lambda hslot: lambda v: v.tensor_mul(
                    h0t[:, hslot * TCH:(hslot + 1) * TCH], silu_t[:],
                    ps_big[:, 512:1024]))(hslot),
                    waits=[tku, tk_sil, tok_h0slot.get(hslot)],
                    inc="h0V")
                tk_hm = P.add("vector", (lambda hslot: lambda v: v.reduce_max(
                    hm1[:], h0t[:, hslot * TCH:(hslot + 1) * TCH],
                    axis=AX.X, apply_absolute_value=True))(hslot),
                    waits=[tk_h0], inc="hmV")
                P.add("vector", (lambda ic: lambda v: v.tensor_max(
                    hmax[:, ic:ic + 1], hmax[:, ic:ic + 1], hm1[:]))(ic),
                    waits=[tk_hm], inc="hmV")
                tok_h0[(ch, ic)] = P.dma("gpsimd",
                    (lambda ch, ic, hslot: lambda g: g.dma_start(
                        out=AP(h0d, ic * 128 * T + ch * TCH, [[T, 128], [1, TCH]]),
                        in_=h0t[:, hslot * TCH:(hslot + 1) * TCH]))(ch, ic, hslot),
                    waits=[tk_h0], inc="dh0a" if hslot == 0 else "dh0b")
                tok_h0slot[hslot] = tok_h0[(ch, ic)]

        P.mark('gateup')
        tk_hs = P.add("vector", lambda v: v.tensor_mul(
            hmax[:], hmax[:], swu_c[:]), waits=[("hmV", P.cnt("hmV")), tk_swu],
            inc="hmV")
        tk_hs2 = P.add("vector", lambda v: v.tensor_mul(
            hmax[:], hmax[:], sxs[:].broadcast_to((128, NIC))),
            waits=[tk_hs], inc="hmV")
        tk_hr = P.add("vector", lambda v: v.reduce_max(
            hmx8[:, 0:1], hmax[:], axis=AX.X), waits=[tk_hs2], inc="hmV")
        tk_hm8 = P.dma("gpsimd", lambda g: g.dma_start(
            out=AP(mx2_in, 0, [[8, 128], [1, 8]]), in_=hmx8[:]),
            waits=[tk_hr], inc="dmx2")
        tk_ar2 = P.add("gpsimd", lambda g: g.collective_compute(
            "AllReduce", AluOpType.max, replica_groups=[list(range(NCORE))],
            ins=[mx2_in.ap()], outs=[mx2_out.ap()]),
            waits=[tk_hm8], inc="agc")
        tk_mxl2 = P.dma("gpsimd", lambda g: g.dma_start(
            out=mxrow[0:1, 0:1024], in_=AP(mx2_out, 0, [[1024, 1], [1, 1024]])),
            waits=[tk_ar2], inc="dmx2")
        tk_gh = P.add("vector", lambda v: v.reduce_max(
            thinsc[0:1, 8:9], mxrow[0:1, 0:1024], axis=AX.X), waits=[tk_mxl2], inc="mxv")
        tk_sg2 = P.add("scalar", lambda s: s.activation(
            thinsc[0:1, 9:10], thinsc[0:1, 2:3], AF.Sigmoid),
            waits=[tk_gh], inc="scE")
        tk_sh1 = P.add("vector", lambda v: v.tensor_mul(
            thinsc[0:1, 10:11], thinsc[0:1, 8:9], thinsc[0:1, 9:10]),
            waits=[tk_sg2], inc="mxv")
        tk_sh2 = P.add("vector", lambda v: v.tensor_scalar_mul(
            thinsc[0:1, 10:11], thinsc[0:1, 10:11], 1.0 / 127.0),
            waits=[tk_sh1], inc="mxv")
        tk_sh3 = P.add("vector", lambda v: v.reciprocal(
            thinsc[0:1, 11:12], thinsc[0:1, 10:11]), waits=[tk_sh2], inc="mxv")
        tkb = P.add("tensor", lambda t: t.matmul(
            ps_e[:, 0:1], onesf[:], thinsc[0:1, 11:12], start=True, stop=True),
            waits=[tk_sh3, ("scE", P.cnt("scE"))], inc="scT")
        tk_shb = P.add("scalar", lambda s: s.copy(shb[:], ps_e[:, 0:1]),
                       waits=[tkb], inc="scE")
        tkb = P.add("tensor", lambda t: t.matmul(
            ps_e[:, 1:2], onesf[:], thinsc[0:1, 10:11], start=True, stop=True),
            waits=[tk_sh3, tk_shb], inc="scT")
        tk_shs = P.add("scalar", lambda s: s.copy(shs[:], ps_e[:, 1:2]),
                       waits=[tkb], inc="scE")
        t1 = P.add("vector", lambda v: v.tensor_mul(
            squcomb[:], swu_c[:], sxs[:].broadcast_to((128, NIC))),
            waits=[tk_swu, tk_sxs], inc="scV")
        tk_squ = P.add("vector", lambda v: v.tensor_mul(
            squcomb[:], squcomb[:], shb[:].broadcast_to((128, NIC))),
            waits=[t1, tk_shb], inc="scV")
        tk_sdc = P.add("vector", lambda v: v.tensor_mul(
            sdcomb[:], swd_c[:], shs[:].broadcast_to((128, 8))),
            waits=[tk_swd, tk_shs], inc="scV")

        P.mark('bar2')
        HT = 256
        for ch in range(NCH):
            for half in range(2):
                tok_qh = {}
                for ic in range(NIC):
                    hslot = ic % 2
                    tl = P.dma("gpsimd", (lambda ch, ic, hslot, half: lambda g: g.dma_start(
                        out=h0t[:, hslot * TCH:hslot * TCH + HT],
                        in_=AP(h0d, ic * 128 * T + ch * TCH + half * HT,
                               [[T, 128], [1, HT]])))(ch, ic, hslot, half),
                        waits=[("dh0a", P.cnt("dh0a")), ("dh0b", P.cnt("dh0b")),
                               ("qhV", max(0, P.cnt("qhV") - 4))],
                        inc="dh0ra" if hslot == 0 else "dh0rb")
                    tq1 = P.add("vector", (lambda ic, hslot: lambda v: v.tensor_scalar_mul(
                        h0q[:, 0:HT], h0t[:, hslot * TCH:hslot * TCH + HT],
                        squcomb[:, ic:ic + 1]))(ic, hslot),
                        waits=[tl, tk_squ, ("dnT", P.cnt("dnT"))], inc="qhV")
                    tq2 = P.add("vector", lambda v: v.tensor_copy(
                        h0qi[:, 0:HT], h0q[:, 0:HT]), waits=[tq1], inc="qhV")
                    tq3 = P.add("vector", lambda v: v.tensor_copy(
                        h0q[:, 0:HT], h0qi[:, 0:HT]), waits=[tq2], inc="qhV")
                    tok_qh[ic] = P.add("vector", (lambda ic: lambda v: v.tensor_scalar(
                        qh[:, ic * HT:(ic + 1) * HT], h0q[:, 0:HT], 127.0, -128.0,
                        AluOpType.min, AluOpType.max))(ic),
                        waits=[tq3], inc="qhV")
                tok_qh_all = ("qhV", P.cnt("qhV"))
                for hc in range(8):
                    dslot2 = hc % 2
                    tdw = P.dma("gpsimd", (lambda hc, dslot2: lambda g: g.dma_start(
                        out=qdt2[:, dslot2 * NIC * 128:(dslot2 + 1) * NIC * 128],
                        in_=AP(agw_out, hc * AG_SZ + AG_QD,
                               [[128, 128], [16384, NIC], [1, 128]])))(hc, dslot2),
                        waits=[("agc", 1), ("dnT", max(0, P.cnt("dnT") - NIC))],
                        inc="ddw0" if dslot2 == 0 else "ddw1")
                    tkd = None
                    for ic in range(NIC):
                        w0 = [tdw, tok_qh_all] if ic == 0 else []
                        if ic == 0:
                            w0.append(("dnE", P.cnt("dnE")))
                        tkd = P.add("tensor", (lambda ic, dslot2, half: lambda t: t.matmul(
                            ps_f[:, half * HT:(half + 1) * HT],
                            qdt2[:, dslot2 * NIC * 128 + ic * 128:
                                 dslot2 * NIC * 128 + (ic + 1) * 128],
                            qh[:, ic * HT:(ic + 1) * HT],
                            start=(ic == 0), stop=(ic == NIC - 1)))(ic, dslot2, half),
                            waits=w0, inc="dnT")
                    tl = P.dma("gpsimd", (lambda ch, hc, half: lambda g: g.dma_start(
                        out=r1t[:, (hc % 2) * TCH:(hc % 2) * TCH + HT],
                        in_=AP(r1d, hc * 128 * T + ch * TCH + half * HT,
                               [[T, 128], [1, HT]])))(ch, hc, half),
                        waits=tok_r1_all + [("dnE", max(0, P.cnt("dnE") - 1))],
                        inc="dq0" if hc % 2 == 0 else "dq1")
                    tsc = P.add("vector", (lambda hc, half: lambda v: v.tensor_scalar_mul(
                        h0q[:, 0:HT], ps_f[:, half * HT:(half + 1) * HT],
                        sdcomb[:, hc:hc + 1]))(hc, half),
                        waits=[tkd, tk_sdc, ("qhV", P.cnt("qhV"))], inc="dnE")
                    P.add("vector", (lambda hc, half: lambda v: v.tensor_add(
                        ubuf[:, hc * TCH + half * HT: hc * TCH + (half + 1) * HT],
                        r1t[:, (hc % 2) * TCH:(hc % 2) * TCH + HT],
                        h0q[:, 0:HT]))(hc, half),
                        waits=[tsc, tl, ("stT", P.cnt("stT")), ("sqS", P.cnt("sqS"))],
                        inc="dnE")
            tok_u2_all = ("dnE", P.cnt("dnE"))
            tkst = None
            for cc in range(8):
                w0 = [tok_u2_all]
                if cc == 0:
                    w0 += [("stE", P.cnt("stE")), ("scE", P.cnt("scE")),
                           ("bcE", P.cnt("bcE"))]
                tkst = P.add("tensor", (lambda cc: lambda t: t.matmul(
                    ps_e[0:1, 0:TCH], onescolb[:],
                    ubuf[:, cc * TCH:(cc + 1) * TCH],
                    start=(cc == 0), stop=(cc == 7)))(cc), waits=w0, inc="stT")
            tk_se1 = P.add("scalar", lambda s: s.copy(
                statr[0:1, 0:TCH], ps_e[0:1, 0:TCH]), waits=[tkst], inc="stE")
            tkst = None
            for cc in range(8):
                tksq = P.add("scalar", (lambda cc: lambda s: s.activation(
                    u2b[:], ubuf[:, cc * TCH:(cc + 1) * TCH], AF.Square))(cc),
                    waits=[tk_se1] if cc == 0 else [("stT", P.cnt("stT"))], inc="sqS")
                tkst = P.add("tensor", (lambda cc: lambda t: t.matmul(
                    ps_e[0:1, 0:TCH], onescolb[:], u2b[:],
                    start=(cc == 0), stop=(cc == 7)))(cc), waits=[tksq], inc="stT")
            tk_se2 = P.add("scalar", lambda s: s.copy(
                statr[0:1, TCH:2 * TCH], ps_e[0:1, 0:TCH]), waits=[tkst], inc="stE")
            tk1 = P.add("vector", lambda v: v.tensor_scalar_mul(
                thin2[0:1, :], statr[0:1, 0:TCH], 1.0 / C), waits=[tk_se1], inc="thV")
            tk2 = P.add("vector", lambda v: v.tensor_scalar_mul(
                thin1[0:1, :], statr[0:1, TCH:2 * TCH], 1.0 / C),
                waits=[tk_se2, tk1], inc="thV")
            tk3 = P.add("vector", lambda v: v.tensor_mul(
                u2b[0:1, 0:TCH], thin2[0:1, :], thin2[0:1, :]), waits=[tk2], inc="thV")
            tk4 = P.add("vector", lambda v: v.tensor_sub(
                thin1[0:1, :], thin1[0:1, :], u2b[0:1, 0:TCH]), waits=[tk3], inc="thV")
            tk5 = P.add("vector", lambda v: v.tensor_scalar_add(
                thin1[0:1, :], thin1[0:1, :], 1e-5), waits=[tk4], inc="thV")
            tk6 = P.add("scalar", lambda s: s.activation(
                thin1[0:1, :], thin1[0:1, :], AF.Sqrt), waits=[tk5], inc="sqS")
            tk7 = P.add("vector", lambda v: v.reciprocal(
                thin1[0:1, :], thin1[0:1, :]), waits=[tk6], inc="thV")
            tk8 = P.add("vector", lambda v: v.tensor_mul(
                thin2[0:1, :], thin2[0:1, :], thin1[0:1, :]), waits=[tk7], inc="thV")
            tk9 = P.add("vector", lambda v: v.tensor_copy(
                thin1b[0:1, :], thin1[0:1, :]), waits=[tk8], inc="thV")
            tk10 = P.add("vector", lambda v: v.tensor_copy(
                thin2b[0:1, :], thin2[0:1, :]), waits=[tk9], inc="thV")
            tb1 = P.add("tensor", lambda t: t.matmul(
                ps_e[:], onesrowb[:], thin1b[0:1, :], start=True, stop=True),
                waits=[tk9, ("bcE", P.cnt("bcE")), ("stE", P.cnt("stE"))], inc="bcT")
            tb1e = P.add("scalar", lambda s: s.copy(bc1s[:], ps_e[:]),
                         waits=[tb1], inc="bcE")
            tb2 = P.add("tensor", lambda t: t.matmul(
                ps_e[:], onesrowb[:], thin2b[0:1, :], start=True, stop=True),
                waits=[tk10, tb1e], inc="bcT")
            tb2e = P.add("scalar", lambda s: s.copy(bc2s[:], ps_e[:]),
                         waits=[tb2], inc="bcE")
            for cc in range(8):
                tkr = P.add("vector", (lambda cc: lambda v: v.tensor_mul(
                    o2t[:, 0:TCH], ubuf[:, cc * TCH:(cc + 1) * TCH], bc1s[:]))(cc),
                    waits=[tb1e, ("dob", P.cnt("dob"))], inc="o2V")
                tkr2 = P.add("vector", lambda v: v.tensor_sub(o2t[:, 0:TCH], o2t[:, 0:TCH], bc2s[:]),
                             waits=[tb2e, tkr], inc="o2V")
                P.dma("gpsimd", (lambda ch, cc: lambda g: g.dma_start(
                    out=AP(obf, cc * 128 * T + ch * TCH, [[T, 128], [1, TCH]]),
                    in_=o2t[:, 0:TCH]))(ch, cc),
                    waits=[tkr2], inc="dob")
        tok_obf_all = ("dob", P.cnt("dob"))

        P.mark('down')
        for ch in range(NCH):
            for tt in range(4):
                for hf2 in range(2):
                    for cc in range(4):
                        ccg = hf2 * 4 + cc
                        P.dma("sync", (lambda ch, tt, ccg, cc: lambda s:
                            s.dma_start_transpose(
                                out=otok[:, cc * 128:(cc + 1) * 128],
                                in_=AP(obf, ccg * 128 * T + ch * TCH + tt * 128,
                                       [[T, 128], [1, 128]])))(ch, tt, ccg, cc),
                            waits=[tok_obf_all, ("ocV", P.cnt("ocV"))], inc="dotr2")
                    tk_oc = P.add("vector", lambda v: v.tensor_copy(ocast[:], otok[:, 0:TCH]),
                                  waits=[("dotr2", P.cnt("dotr2")),
                                         ("dos", P.cnt("dos"))], inc="ocV")
                    P.dma("gpsimd", (lambda ch, tt, hf2: lambda g: g.dma_start(
                        out=out_t.ap()[ch * TCH + tt * 128: ch * TCH + (tt + 1) * 128,
                                       hf2 * 512:(hf2 + 1) * 512],
                        in_=ocast[:]))(ch, tt, hf2),
                        waits=[tk_oc], inc="dos")

        dbgout("dispF", [128, 4 * (64 + SCW)])
        dbgout("etile", [64, 16 * TCH])
        dbgout("att", [128, 8 * TCH])
        dbgout("qh", [128, NIC * 256])
        dbgout("fh0", [T, FROW], BF16)
        if "fh0" in dbg:
            P.dma("gpsimd", lambda g: g.dma_start(
                out=dbg["fh0"].ap(),
                in_=AP(fbuf, FOFF, [[FROW, T], [1, FROW]])),
                waits=[("h1a", P.cnt("h1a")), ("h1b", P.cnt("h1b"))], inc="dbgs")
        if "dispF" in dbg:
            P.dma("gpsimd", lambda g: g.dma_start(
                out=dbg["dispF"].ap(), in_=dispF[:]),
                waits=[("a1E", P.cnt("a1E"))], inc="dbgs")
        if "etile" in dbg:
            P.dma("gpsimd", lambda g: g.dma_start(
                out=dbg["etile"].ap(), in_=etile[:]),
                waits=[("enV", P.cnt("enV"))], inc="dbgs")
        if "att" in dbg:
            P.dma("gpsimd", lambda g: g.dma_start(
                out=dbg["att"].ap(), in_=att[:]),
                waits=[("vwE", P.cnt("vwE"))], inc="dbgs")
        if "qh" in dbg:
            P.dma("gpsimd", lambda g: g.dma_start(
                out=dbg["qh"].ap(), in_=qh[:, 0:NIC * 256]),
                waits=[("qhV", P.cnt("qhV"))], inc="dbgs")

        P.add("gpsimd", lambda g: g.nop(),
              waits=[("dos", P.cnt("dos")), ("dbgs", P.cnt("dbgs"))])
        P.add("sync", lambda s: s.nop(),
              waits=[("dos", P.cnt("dos"))])

        if limit is not None:
            P.ops = P.ops[:limit]
            allw = [(k, v) for k, v in P.count.items()]
            cnt2 = {}
            for _e, _f, _w, _i in P.ops:
                for _s, _a in _i:
                    cnt2[_s] = cnt2.get(_s, 0) + _a
            P.count = cnt2
            for _e, _f, _w, _i in P.ops:
                for wi in range(len(_w)):
                    pass
            newops = []
            for _e, _f, _w, _i in P.ops:
                _w2 = [(s2, v2) for (s2, v2) in _w if cnt2.get(s2, 0) >= v2]
                newops.append((_e, _f, _w2, _i))
            P.ops = newops
            qw = [(k, v) for k, v in cnt2.items()]
            for _eng in ("sync", "vector", "scalar", "tensor", "gpsimd"):
                P.ops.append((_eng, (lambda e: e.nop()), qw, []))
        P.check()
        P.emit(blk, sems)
    return nc, dbg


_NC_CACHE = {}


def _get_nc(debug=()):
    key = tuple(sorted(debug))
    if key not in _NC_CACHE:
        _NC_CACHE[key] = build(debug)
    return _NC_CACHE[key]


def kernel(**inputs):
    x = np.asarray(inputs["L0_attn_x"], np.float32)
    wg = np.asarray(inputs["L0_ffn_w_gate"], np.float32)
    wu = np.asarray(inputs["L0_ffn_w_up"], np.float32)
    wd = np.asarray(inputs["L0_ffn_w_down"], np.float32)
    wg_p = np.ones((NCORE * ISH, C), np.float32)
    wg_p[:FFI] = wg
    wu_p = np.ones((NCORE * ISH, C), np.float32)
    wu_p[:FFI] = wu

    common = {
        "wdisp": np.asarray(inputs["L0_attn_W_disp"], np.float32),
        "wval": np.asarray(inputs["L0_attn_W_val"], np.float32),
        "relpos": np.asarray(inputs["L0_attn_rel_pos_emb"], np.float32),
        "sfw": np.asarray(inputs["L0_attn_strain_fused_w"], np.float32),
        "sfb": np.asarray(inputs["L0_attn_strain_fused_b"], np.float32),
        "ppw": np.asarray(inputs["L0_attn_pos_proj_w"], np.float32),
        "wbond": np.asarray(inputs["L0_attn_bond_out_w"], np.float32),
        "wdmg": np.asarray(inputs["L0_attn_damage_out_w"], np.float32),
        "dmgb": np.asarray(inputs["L0_attn_damage_out_b"], np.float32),
        "wcproj": np.asarray(inputs["L0_attn_c_proj_w"], np.float32),
        "clip_x": np.asarray(inputs["L0_ffn_clip_x"], np.float32),
        "clip_dn": np.asarray(inputs["L0_ffn_clip_down"], np.float32),
    }
    in_maps = []
    for r in range(NCORE):
        m = dict(common)
        m["x"] = np.ascontiguousarray(x[r])
        m["wg_sh"] = np.ascontiguousarray(wg_p[r * ISH:(r + 1) * ISH])
        m["wu_sh"] = np.ascontiguousarray(wu_p[r * ISH:(r + 1) * ISH])
        m["wd_sh"] = np.ascontiguousarray(wd[r * 128:(r + 1) * 128])
        in_maps.append(m)
    nc, _ = _get_nc()
    res = run_bass_kernel_spmd(nc, in_maps, core_ids=list(range(NCORE)))
    return np.stack([np.asarray(res.results[r]["out"]) for r in range(NCORE)])


# version 11
# speedup vs baseline: 1.0083x; 1.0083x over previous
import numpy as np
import ml_dtypes
from contextlib import ExitStack

import concourse.bass as bass
import concourse.mybir as mybir
from concourse.bass_utils import run_bass_kernel_spmd
from concourse.alu_op_type import AluOpType

F32 = mybir.dt.float32
BF16 = mybir.dt.bfloat16
I32 = mybir.dt.int32
AF = mybir.ActivationFunctionType
AX = mybir.AxisListType
AP = bass.AP

T = 4096
C = 1024
NH = 16
FFI = 2816
NIC = 22
ISH = 384
TCH = 512
NCH = 8
NSC = 2
SCW = 2048
FROW = 256
FOFF = 512
FHSZ = T * FROW + 2 * FOFF
AG_QG = 0
AG_QU = 1024 * 384
AG_QD = 2 * 1024 * 384
AG_SZ = AG_QD + FFI * 128
NCORE = 8


class Prog:
    def __init__(self):
        self.ops = []
        self.count = {}

    def cnt(self, sem):
        return self.count.get(sem, 0)

    def add(self, eng, fn, waits=(), inc=None, amt=1, inc2=None, amt2=1):
        incs = []
        for s, a in ((inc, amt), (inc2, amt2)):
            if s is not None:
                self.count[s] = self.count.get(s, 0) + a
                incs.append((s, a))
        self.ops.append((eng, fn, [w for w in waits if w is not None], incs))
        return (inc, self.count[inc]) if inc else None

    def dma(self, eng, fn, waits=(), inc=None):
        return self.add(eng, fn, waits, inc, amt=16)

    def mark(self, name):
        if not hasattr(self, "marks"):
            self.marks = []
        self.marks.append((name, len(self.ops)))

    def check(self):
        for eng, fn, waits, incs in self.ops:
            for s, v in waits:
                assert self.count.get(s, 0) >= v, \
                    f"wait {s}>={v} unreachable (max {self.count.get(s, 0)})"

    def emit(self, block, sems):
        def mk(name):
            def body(eng):
                for e, fn, waits, incs in self.ops:
                    if e != name:
                        continue
                    for s, v in waits:
                        eng.wait_ge(sems[s], v)
                    inst = fn(eng)
                    if incs:
                        inst.then_inc(sems[incs[0][0]], incs[0][1])
                        for s, a in incs[1:]:
                            eng.nop().then_inc(sems[s], a)
            return body
        block.sync(mk("sync"))
        block.vector(mk("vector"))
        block.scalar(mk("scalar"))
        block.tensor(mk("tensor"))
        block.gpsimd(mk("gpsimd"))


def build(debug=(), limit=None):
    nc = bass.Bass("TRN2", target_bir_lowering=False)
    bfe = ml_dtypes.bfloat16

    def inp(name, shape):
        return nc.dram_tensor(name, list(shape), F32, kind="ExternalInput")

    x_in = inp("x", [T, C])
    wdisp_in = inp("wdisp", [256, C])
    wval_in = inp("wval", [C, C])
    relpos_in = inp("relpos", [64, 16])
    sfw_in = inp("sfw", [32, 16])
    sfb_in = inp("sfb", [32])
    ppw_in = inp("ppw", [16, 16])
    wbond_in = inp("wbond", [1, 16])
    wdmg_in = inp("wdmg", [1, 16])
    dmgb_in = inp("dmgb", [1])
    wcproj_in = inp("wcproj", [C, C])
    clipx_in = inp("clip_x", [1])
    clipd_in = inp("clip_dn", [1])
    wg_in = inp("wg_sh", [ISH, C])
    wu_in = inp("wu_sh", [ISH, C])
    wd_in = inp("wd_sh", [128, FFI])

    out_t = nc.dram_tensor("out", [T, C], F32, kind="ExternalOutput")
    dbg = {}
    def dbgout(name, shape, dtype=F32):
        if name in debug:
            dbg[name] = nc.dram_tensor("dbg_" + name, list(shape), dtype,
                                       kind="ExternalOutput")
        return dbg.get(name)

    xbf = nc.dram_tensor("xbf", [T * C], BF16)
    wvbf = nc.dram_tensor("wvbf", [C * C], BF16)
    wcbf = nc.dram_tensor("wcbf", [C * C], BF16)
    fbuf = nc.dram_tensor("fbuf", [NH * FHSZ], BF16)
    obf = nc.dram_tensor("obf", [C * T], BF16)
    r1d = nc.dram_tensor("r1d", [C * T], BF16)
    h0d = nc.dram_tensor("h0d", [NIC * 128 * T], BF16)
    agw_in = nc.dram_tensor("agw_in", [AG_SZ], BF16)
    agw_out = nc.dram_tensor("agw_out", [NCORE * AG_SZ], BF16)
    ags_in = nc.dram_tensor("ags_in", [1024], BF16)
    ags_out = nc.dram_tensor("ags_out", [NCORE * 1024], BF16)
    mx1_in = nc.dram_tensor("mx1_in", [1024], F32)
    mx1_out = nc.dram_tensor("mx1_out", [1024], F32)
    mx2_in = nc.dram_tensor("mx2_in", [1024], F32)
    mx2_out = nc.dram_tensor("mx2_out", [1024], F32)

    identD = nc.inline_tensor(np.eye(128, dtype=bfe), "identD")
    identFD = nc.inline_tensor(np.eye(128, dtype=np.float32), "identFD")
    p_np = np.zeros((128, 64), dtype=bfe)
    for m in range(64):
        p_np[64 + m, m] = 1
    prelD = nc.inline_tensor(p_np, "prelD")
    ind4_np = np.zeros((64, 4), dtype=bfe)
    for p in range(64):
        ind4_np[p, p // 16] = 1
    ind4D = nc.inline_tensor(ind4_np, "ind4D")
    sel4D = nc.inline_tensor(np.ascontiguousarray(ind4_np.T), "sel4D")
    onesFD = nc.inline_tensor(np.ones((1, 128), dtype=np.float32), "onesFD")
    selb_np = np.zeros((16, 8 * 128), dtype=bfe)
    for h in range(4):
        for f in range(16):
            selb_np[f, h * 128 + 32 * h + f] = 1
            selb_np[f, (4 + h) * 128 + 32 * h + 16 + f] = 1
    selbD = nc.inline_tensor(selb_np, "selbD")
    rep32_np = np.zeros((32, 128), dtype=np.float32)
    for k in range(32):
        for h in range(4):
            rep32_np[k, 32 * h + k] = 1
    rep32D = nc.inline_tensor(rep32_np, "rep32D")
    rep16_np = np.zeros((16, 128), dtype=np.float32)
    for f in range(16):
        for h in range(4):
            rep16_np[f, 32 * h + f] = 1
    rep16D = nc.inline_tensor(rep16_np, "rep16D")
    cm_np = np.zeros((64, 4 * TCH), dtype=bfe)
    for wg in range(4):
        for p in range(64):
            w = 16 * wg + (p % 16)
            cm_np[p, wg * TCH + max(0, 63 - w):(wg + 1) * TCH] = 1
    cmaskD = nc.inline_tensor(cm_np, "cmaskD")

    st = ExitStack()
    SB = lambda n, s, d=BF16: st.enter_context(nc.sbuf_tensor(n, list(s), d))
    PT = lambda n, s, d=F32: st.enter_context(nc.psum_tensor(n, list(s), d))

    wcombT = SB("wcombT", [128, 8 * 512])
    wvT = SB("wvT", [128, 8 * 1024])
    wcT2 = SB("wcT2", [128, 2 * 1024])
    sqbuf = SB("sqbuf", [128, 160])
    sqabs = SB("sqabs", [128, 64])
    ctile = SB("ctile", [128, 64], F32)
    cmaxc = SB("cmaxc", [128, 1], F32)
    ident = SB("ident_s", [128, 128])
    identf = SB("identf_s", [128, 128], F32)
    prel = SB("prel_s", [128, 64])
    ind4 = SB("ind4_s", [64, 4])
    sel4 = SB("sel4_s", [4, 64])
    onesf = SB("onesf_s", [1, 128], F32)
    onescolb = SB("onescolb_s", [128, 1])
    onesrowb = SB("onesrowb_s", [1, 128])
    cmask = SB("cmask_s", [64, 4 * TCH])
    selbs = SB("selbs_s", [16, 8 * 128])
    rep32s = SB("rep32_s", [32, 128], F32)
    rep16s = SB("rep16_s", [16, 128], F32)
    negb = SB("negb", [64, 4 * NSC], F32)
    dmgbc = SB("dmgbc", [64, 1], F32)
    zerot = SB("zerot", [128, 512])
    xa1 = SB("xa1", [128, 2 * TCH])
    xa2 = SB("xa2", [128, 4 * TCH])
    xur = SB("xur", [128, 2 * TCH])
    dispF = SB("dispF", [128, 4 * (64 + SCW)])
    valr = SB("valr", [128, 5 * 1024])
    gout = SB("gout", [128, 2 * SCW])
    dsl = SB("dsl", [128, 2 * TCH])
    e1t = SB("e1t", [64, TCH])
    e2t = SB("e2t", [64, TCH])
    etile = SB("etile", [64, 16 * TCH])
    rrow = SB("rrow", [4, TCH], F32)
    rrowb = SB("rrowb", [4, TCH])
    rbev = SB("rbev", [64, TCH])
    ent = SB("ent", [128, 2 * 256])
    ebt = SB("ebt", [128, 4 * 128])
    att = SB("att", [128, 8 * TCH])
    ubuf = SB("ubuf", [128, 8 * TCH])
    u2b = SB("u2b", [128, TCH])
    statr = SB("statr", [1, 2 * TCH], F32)
    thin1 = SB("thin1", [1, TCH], F32)
    thin2 = SB("thin2", [1, TCH], F32)
    thin1b = SB("thin1b", [1, TCH])
    thin2b = SB("thin2b", [1, TCH])
    bc1s = SB("bc1s", [128, TCH])
    bc2s = SB("bc2s", [128, TCH])
    m2t = SB("m2t", [128, 1], F32)
    m2bb = SB("m2bb", [128, 1])
    wqf = SB("wqf", [128, 1024], F32)
    wqi = SB("wqi", [128, 1024], I32)
    wqb = SB("wqb", [128, 1024])
    wqtb = SB("wqtb", [128, 8 * 384])
    wdf = SB("wdf", [128, 1024], F32)
    wdi = SB("wdi", [128, 1024], I32)
    wdb = SB("wdb", [128, FFI])
    wdtb = SB("wdtb", [128, NIC * 128])
    wsc = SB("wsc", [128, 8], F32)
    scs8 = SB("scs8", [3, 384])
    scbuf = SB("scbuf", [1, 1536])
    r1t = SB("r1t", [128, 2 * TCH])
    qx = ubuf
    qxf = SB("qxf", [128, TCH], F32)
    qxi = SB("qxi", [128, TCH], I32)
    qgt2 = SB("qgt2", [128, 2 * 8 * 128])
    qut2 = SB("qut2", [128, 2 * 8 * 128])
    qdt2 = SB("qdt2", [128, 2 * NIC * 128])
    silu_t = SB("silu_t", [128, TCH], F32)
    h0t = SB("h0t", [128, 2 * TCH])
    h0q = qxf
    h0qi = qxi
    qh = dispF
    hmax = SB("hmax", [128, NIC], F32)
    hm1 = SB("hm1", [128, 1], F32)
    xmax8 = SB("xmax8", [128, 8], F32)
    hmx8 = SB("hmx8", [128, 8], F32)
    mxrow = statr
    thinsc = SB("thinsc", [1, 16], F32)
    sxb = SB("sxb", [128, 1], F32)
    sxs = SB("sxs", [128, 1], F32)
    shb = SB("shb", [128, 1], F32)
    shs = SB("shs", [128, 1], F32)
    swg_c = SB("swg_c", [128, NIC], F32)
    swu_c = SB("swu_c", [128, NIC], F32)
    swd_c = SB("swd_c", [128, 8], F32)
    sgcomb = SB("sgcomb", [128, NIC], F32)
    squcomb = SB("squcomb", [128, NIC], F32)
    sdcomb = SB("sdcomb", [128, 8], F32)
    ocast = silu_t
    otok = ebt
    o2t = dsl
    wdisp_bd = SB("wdisp_bd", [16, 2 * 128])
    sfwb = SB("sfwb", [32, 16])
    relb = SB("relb", [64, 16], F32)
    relT = SB("relT", [16, 64], F32)
    ppwb = SB("ppwb", [16, 16], F32)
    ppwT = SB("ppwT", [16, 16], F32)
    sfwT = SB("sfwT", [16, 32])
    posrow = SB("posrow", [16, 64], F32)
    sfbrow = SB("sfbrow", [1, 32], F32)
    sfbcol = SB("sfbcol", [32, 1], F32)
    sfbbc = SB("sfbbc", [128, 1], F32)
    wbrow = SB("wbrow", [1, 16])
    wdrow = SB("wdrow", [1, 16])
    wbcol = SB("wbcol", [16, 1])
    wdcol = SB("wdcol", [16, 1])
    wcol8 = SB("wcol8", [128, 8])
    wcol4a = SB("wcol4a", [128, 4], F32)

    psd = [PT(f"psd{i}", [128, TCH]) for i in range(4)]
    ps_e = PT("ps_e", [128, TCH])
    ps_f = PT("ps_f", [128, TCH])
    ps_big = PT("ps_big", [128, 1024])

    sem_names = [
        "dcast", "dzero", "dcons", "dsml", "vz", "dwvt", "dwct0", "dwct1",
        "dxa1a", "dxa1b", "dxua", "dxub", "h1a", "h1b", "dr1a", "dr1b2",
        "dgw0", "dgw1", "dgu0", "dgu1", "ddw0", "ddw1", "dq0", "dq1",
        "dh0a", "dh0b", "dh0ra", "dh0rb", "dwda", "dwdb",
        "h2s0", "h2s1", "h2s2", "h2s3",
        "suT", "suE", "suV",
        "dxa2", "a1T", "a1E", "valT", "valE",
        "bmT", "bmE", "m2V",
        "gsub", "gelu", "gfree", "dotr", "dotf", "sigS", "relT_", "e2S",
        "ecmb", "smmT", "rrV", "rbT", "rbE", "enV", "entT", "entE",
        "vwT", "vwE", "cpT", "uV", "sqS", "stT", "stE",
        "thV", "bcT", "bcE", "r1V", "dr1",
        "dwq", "wqV", "wqT", "wqE", "dagw", "agc", "dscl",
        "dmx1", "mxv", "scT", "scE", "scV",
        "dr1b", "qxV", "guT", "silS", "h0V", "hmV",
        "dmx2",
        "qhV", "dnT", "dnE",
        "o2V", "dob", "dotr2", "ocV", "dos",
        "dbgs",
    ]

    P = Prog()
    blk = st.enter_context(nc.Block())
    sems = {s: st.enter_context(nc.semaphore(s)) for s in sem_names}

    def dcol(pack, tl):
        return pack * (64 + SCW) + 64 + tl

    def fh(h):
        return h * FHSZ

    eb = lambda a: a.bitcast(BF16)

    with st:
        tk_xc0 = P.dma("gpsimd", lambda g: g.dma_start(
            out=AP(xbf, 0, [[C, T], [1, C]]), in_=x_in.ap()), inc="dcast")
        tk_wv = P.dma("gpsimd", lambda g: g.dma_start(
            out=AP(wvbf, 0, [[C, C], [1, C]]), in_=wval_in.ap()), inc="dcast")
        tk_wc = P.dma("gpsimd", lambda g: g.dma_start(
            out=AP(wcbf, 0, [[C, C], [1, C]]), in_=wcproj_in.ap()), inc="dcast")
        tk_xc = tk_wv = tk_wc = ("dcast", 48)

        tk_vz = P.add("vector", lambda v: v.memset(zerot[:], 0.0), inc="vz")
        P.add("vector", lambda v: v.memset(onescolb[:], 1.0), inc="vz")
        P.add("vector", lambda v: v.memset(onesrowb[:], 1.0), inc="vz")
        P.add("vector", lambda v: v.memset(sqbuf[:], 0.0), inc="vz")
        P.add("vector", lambda v: v.memset(xmax8[:], 0.0), inc="vz")
        P.add("vector", lambda v: v.memset(hmx8[:], 0.0), inc="vz")
        P.add("vector", lambda v: v.memset(hmax[:], 0.0), inc="vz")
        P.add("vector", lambda v: v.memset(dsl[:], 0.0), inc="vz")
        tk_vz_all = ("vz", P.cnt("vz"))

        FTOT = NH * FHSZ
        for off in range(0, FTOT, 128 * 512):
            n = min(128 * 512, FTOT - off)
            rows, rem = n // 512, n % 512
            if rows:
                P.dma("gpsimd", (lambda o, r: lambda g: g.dma_start(
                    out=AP(fbuf, o, [[512, r], [1, 512]]),
                    in_=zerot[0:r, :]))(off, rows), waits=[tk_vz], inc="dzero")
            if rem:
                P.dma("gpsimd", (lambda o, r: lambda g: g.dma_start(
                    out=AP(fbuf, o, [[r, 1], [1, r]]),
                    in_=zerot[0:1, 0:r]))(off + rows * 512, rem),
                    waits=[tk_vz], inc="dzero")
        tk_fz = ("dzero", P.cnt("dzero"))

        for dst, src in [(ident, identD), (identf, identFD), (prel, prelD),
                         (ind4, ind4D), (sel4, sel4D), (onesf, onesFD),
                         (cmask, cmaskD), (selbs, selbD), (rep32s, rep32D),
                         (rep16s, rep16D)]:
            P.dma("gpsimd", (lambda d, s: lambda g: g.dma_start(
                out=d[:], in_=s.ap()))(dst, src), inc="dcons")
        tk_cons = ("dcons", P.cnt("dcons"))

        for fn in [
            lambda g: g.dma_start(out=sfwb[:], in_=sfw_in.ap()),
            lambda g: g.dma_start(out=relb[:], in_=relpos_in.ap()),
            lambda g: g.dma_start(out=ppwb[:], in_=ppw_in.ap()),
            lambda g: g.dma_start(out=sfbrow[:], in_=AP(sfb_in, 0, [[32, 1], [1, 32]])),
            lambda g: g.dma_start(out=wbrow[:], in_=wbond_in.ap()),
            lambda g: g.dma_start(out=wdrow[:], in_=wdmg_in.ap()),
            lambda g: g.dma_start(out=thinsc[0:1, 0:1], in_=AP(dmgb_in, 0, [[1, 1], [1, 1]])),
            lambda g: g.dma_start(out=thinsc[0:1, 1:2], in_=AP(clipx_in, 0, [[1, 1], [1, 1]])),
            lambda g: g.dma_start(out=thinsc[0:1, 2:3], in_=AP(clipd_in, 0, [[1, 1], [1, 1]])),
        ]:
            P.dma("gpsimd", fn, inc="dsml")
        tk_sml = ("dsml", P.cnt("dsml"))

        for cc in range(8):
            for vv in range(8):
                P.dma("sync", (lambda cc, vv: lambda s: s.dma_start_transpose(
                    out=wvT[:, cc * 1024 + vv * 128: cc * 1024 + (vv + 1) * 128],
                    in_=AP(wvbf, vv * 128 * C + cc * 128, [[C, 128], [1, 128]])))(cc, vv),
                    waits=[tk_wv], inc="dwvt")
        tk_wvT = ("dwvt", P.cnt("dwvt"))

        P.mark('init')
        tk = P.add("tensor", lambda t: t.transpose(
            eb(ps_e[:])[0:16, 0:32], sfwb[0:32, 0:16], ident[0:32, 0:32]),
            waits=[tk_sml, tk_cons], inc="suT")
        tk_sfwT = P.add("scalar", lambda s: s.copy(sfwT[:], eb(ps_e[:])[0:16, 0:32]),
                        waits=[tk], inc="suE")
        tk = P.add("tensor", lambda t: t.transpose(ps_f[0:16, 0:64], relb[:], identf[0:64, 0:64]),
                   waits=[tk_sml, tk_cons, tk_sfwT], inc="suT")
        tk_relT = P.add("scalar", lambda s: s.copy(relT[:], ps_f[0:16, 0:64]),
                        waits=[tk], inc="suE")
        tk = P.add("tensor", lambda t: t.transpose(ps_f[0:16, 64:80], ppwb[:], identf[0:16, 0:16]),
                   waits=[tk_relT], inc="suT")
        tk_ppwT = P.add("scalar", lambda s: s.copy(ppwT[:], ps_f[0:16, 64:80]),
                        waits=[tk], inc="suE")
        tk = P.add("tensor", lambda t: t.matmul(ps_f[0:16, 0:64], ppwT[:], relT[:],
                                                start=True, stop=True),
                   waits=[tk_ppwT], inc="suT")
        tk_pos = P.add("scalar", lambda s: s.copy(posrow[:], ps_f[0:16, 0:64]),
                       waits=[tk], inc="suE")
        tk = P.add("tensor", lambda t: t.transpose(ps_f[0:32, 96:97], sfbrow[:], identf[0:1, 0:1]),
                   waits=[tk_pos], inc="suT")
        tk_sfbc = P.add("scalar", lambda s: s.copy(sfbcol[:], ps_f[0:32, 96:97]),
                        waits=[tk], inc="suE")
        tk = P.add("tensor", lambda t: t.matmul(ps_f[:, 100:101], rep32s[:], sfbcol[:],
                                                start=True, stop=True),
                   waits=[tk_sfbc], inc="suT")
        tk_sfbb = P.add("scalar", lambda s: s.copy(sfbbc[:], ps_f[:, 100:101]),
                        waits=[tk], inc="suE")
        tk = P.add("tensor", lambda t: t.matmul(ps_f[:, 0:64], rep16s[:], posrow[:],
                                                start=True, stop=True),
                   waits=[tk_sfbb], inc="suT")
        tk_ct = P.add("vector", lambda v: v.tensor_add(
            ctile[:], ps_f[:, 0:64], sfbbc[:].broadcast_to((128, 64))),
            waits=[tk], inc="suV")
        tk_cm = P.add("vector", lambda v: v.reduce_max(
            cmaxc[:], ctile[:], axis=AX.X, apply_absolute_value=True),
            waits=[tk_ct], inc="suV")
        tk_cm2 = P.add("vector", lambda v: v.tensor_scalar_add(cmaxc[:], cmaxc[:], 0.17),
                       waits=[tk_cm], inc="suV")
        tk = P.add("tensor", lambda t: t.matmul(ps_e[0:16, 0:1], wbrow[:],
                                                onescolb[0:1, 0:1], start=True, stop=True),
                   waits=[tk_sml, tk_vz_all, tk_sfwT], inc="suT")
        tk_wbc = P.add("scalar", lambda s: s.copy(wbcol[:], ps_e[0:16, 0:1]),
                       waits=[tk], inc="suE")
        tk = P.add("tensor", lambda t: t.matmul(ps_e[0:16, 1:2], wdrow[:],
                                                onescolb[0:1, 0:1], start=True, stop=True),
                   waits=[tk_wbc], inc="suT")
        tk_wdc = P.add("scalar", lambda s: s.copy(wdcol[:], ps_e[0:16, 1:2]),
                       waits=[tk], inc="suE")
        tk = None
        for k in range(8):
            col = wbcol if k < 4 else wdcol
            tk = P.add("tensor", (lambda k, col: lambda t: t.matmul(
                ps_e[:, 8 + k:9 + k], selbs[:, 128 * k:128 * (k + 1)], col[:],
                start=True, stop=True))(k, col), waits=[tk_wdc], inc="suT")
        tk_w8 = P.add("scalar", lambda s: s.copy(wcol8[:], ps_e[:, 8:16]),
                      waits=[tk], inc="suE")
        tk_w4a = P.add("scalar", lambda s: s.activation(
            wcol4a[:], ps_e[:, 8:12], AF.Abs), waits=[tk], inc="suE")
        tk_sq = P.add("scalar", lambda s: s.copy(
            AP(sqbuf, 15, [[160, 128], [16, 8]]),
            AP(wcol8, 0, [[8, 128], [1, 8]])),
            waits=[tk_w8, tk_vz_all], inc="suE")
        tk_sqa = P.add("scalar", lambda s: s.copy(
            AP(sqabs, 0, [[64, 128], [16, 4], [1, 16]]),
            AP(wcol4a, 0, [[4, 128], [1, 4], [0, 16]])),
            waits=[tk_w4a], inc="suE")
        tk = P.add("tensor", lambda t: t.matmul(ps_f[0:64, 102:103], onesf[0:1, 0:64],
                                                thinsc[0:1, 0:1], start=True, stop=True),
                   waits=[tk_sml, tk_sfbb], inc="suT")
        tk_dbc = P.add("scalar", lambda s: s.copy(dmgbc[:], ps_f[0:64, 102:103]),
                       waits=[tk], inc="suE")
        tk_wct2 = None
        wdctr = [0]
        for cc in range(8):
            tkm = None
            for h in range(NH):
                wsl = wdctr[0] % 2
                tk_wl = P.dma("gpsimd", (lambda cc, h, wsl: lambda g: g.dma_start(
                    out=wdisp_bd[:, wsl * 128:(wsl + 1) * 128],
                    in_=AP(wdisp_in, (h * 16) * C + cc * 128,
                           [[C, 16], [1, 128]])))(cc, h, wsl),
                    waits=[("suT", max(0, P.cnt("suT") - 1))],
                    inc="dwda" if wsl == 0 else "dwdb")
                wdctr[0] += 1
                tkm = P.add("tensor", (lambda cc, h, wsl: lambda t: t.matmul(
                    ps_f[:, 32 * h:32 * (h + 1)],
                    wdisp_bd[:, wsl * 128:(wsl + 1) * 128],
                    sfwT[:], start=True, stop=True))(cc, h, wsl),
                    waits=[tk_wl] + ([tk_sfwT, tk_dbc] if (h == 0 and cc == 0) else []) +
                          ([("suE", P.cnt("suE"))] if h == 0 else []),
                    inc="suT")
            tk_wct2 = P.add("scalar", (lambda cc: lambda s: s.copy(
                wcombT[:, cc * 512:(cc + 1) * 512], ps_f[:]))(cc),
                waits=[tkm], inc="suE")
        tk_setup = tk_wct2

        P.mark('setup')
        def wq_gateup(src_in, scrow_idx, waits0):
            tk_q = tk_T = None
            for kb in range(3):
                tk_l = P.dma("gpsimd", (lambda kb, src_in: lambda g: g.dma_start(
                    out=wqf[:],
                    in_=src_in.ap()[kb * 128:(kb + 1) * 128, :]))(kb, src_in),
                    waits=waits0 + [("wqV", P.cnt("wqV")), ("wqT", P.cnt("wqT"))],
                    inc="dwq")
                tk_m = P.add("vector", (lambda kb: lambda v: v.reduce_max(
                    wsc[:, kb:kb + 1], wqf[:], axis=AX.X,
                    apply_absolute_value=True))(kb),
                    waits=[tk_l], inc="wqV")
                t1 = P.add("vector", (lambda kb: lambda v: v.reciprocal(
                    wsc[:, 6:7], wsc[:, kb:kb + 1]))(kb), waits=[tk_m], inc="wqV")
                t2 = P.add("vector", lambda v: v.tensor_scalar_mul(
                    wsc[:, 6:7], wsc[:, 6:7], 127.0), waits=[t1], inc="wqV")
                t3 = P.add("vector", lambda v: v.tensor_scalar_mul(
                    wqf[:], wqf[:], wsc[:, 6:7]), waits=[t2], inc="wqV")
                t4 = P.add("vector", lambda v: v.tensor_copy(wqi[:], wqf[:]),
                           waits=[t3], inc="wqV")
                tk_q = P.add("vector", lambda v: v.tensor_copy(wqb[:], wqi[:]),
                             waits=[t4], inc="wqV")
                for hc in range(8):
                    tkt = P.add("tensor", (lambda hc: lambda t: t.transpose(
                        eb(ps_e[:])[:, 0:128],
                        wqb[:, hc * 128:(hc + 1) * 128], ident[:]))(hc),
                        waits=[tk_q, ("wqE", P.cnt("wqE"))], inc="wqT")
                    tk_T = P.add("scalar", (lambda hc, kb: lambda s: s.copy(
                        wqtb[:, hc * 384 + kb * 128: hc * 384 + (kb + 1) * 128],
                        eb(ps_e[:])[:, 0:128]))(hc, kb),
                        waits=[tkt], inc="wqE")
            tkt = P.add("tensor", lambda t: t.transpose(
                ps_f[0:3, 0:128], wsc[:, 0:3], identf[:]),
                waits=[tk_q, tk_setup], inc="wqT")
            tk_S = P.add("scalar", (lambda i0: lambda s: s.activation(
                scs8[0:3, i0 * 128 // 3 if False else (0 if i0 == 0 else 128):
                     (128 if i0 == 0 else 256)],
                ps_f[0:3, 0:128], AF.Copy,
                scale=1.0 / 127.0))(scrow_idx), waits=[tkt], inc="wqE")
            return tk_q, tk_T, tk_S

        tk_gq, tk_gT, tk_gS = wq_gateup(wg_in, 0, [tk_setup])
        tk_gW = P.dma("gpsimd", lambda g: g.dma_start(
            out=AP(agw_in, AG_QG, [[384, 128], [49152, 8], [1, 384]]),
            in_=AP(wqtb, 0, [[3072, 128], [384, 8], [1, 384]])),
            waits=[tk_gT], inc="dagw")
        tk_uq, tk_uT, tk_uS = wq_gateup(wu_in, 3, [tk_gW, tk_gS])
        tk_uW = P.dma("gpsimd", lambda g: g.dma_start(
            out=AP(agw_in, AG_QU, [[384, 128], [49152, 8], [1, 384]]),
            in_=AP(wqtb, 0, [[3072, 128], [384, 8], [1, 384]])),
            waits=[tk_uT], inc="dagw")
        tk_dm = None
        for pc in range(3):
            w_ = min(1024, FFI - pc * 1024)
            tk_dl = P.dma("gpsimd", (lambda pc, w_: lambda g: g.dma_start(
                out=wdf[:, 0:w_], in_=wd_in.ap()[:, pc * 1024:pc * 1024 + w_]))(pc, w_),
                waits=[tk_uq, ("wqV", P.cnt("wqV"))], inc="dwq")
            tk_r = P.add("vector", (lambda w_: lambda v: v.reduce_max(
                wsc[:, 5:6], wdf[:, 0:w_], axis=AX.X,
                apply_absolute_value=True))(w_), waits=[tk_dl], inc="wqV")
            if pc == 0:
                tk_dm = P.add("vector", lambda v: v.tensor_copy(
                    wsc[:, 6:7], wsc[:, 5:6]), waits=[tk_r], inc="wqV")
            else:
                tk_dm = P.add("vector", lambda v: v.tensor_max(
                    wsc[:, 6:7], wsc[:, 6:7], wsc[:, 5:6]), waits=[tk_r], inc="wqV")
        t1 = P.add("vector", lambda v: v.reciprocal(wsc[:, 7:8], wsc[:, 6:7]),
                   waits=[tk_dm], inc="wqV")
        t2 = P.add("vector", lambda v: v.tensor_scalar_mul(wsc[:, 7:8], wsc[:, 7:8], 127.0),
                   waits=[t1], inc="wqV")
        tk_dq = None
        for pc in range(3):
            w_ = min(1024, FFI - pc * 1024)
            tk_dl = P.dma("gpsimd", (lambda pc, w_: lambda g: g.dma_start(
                out=wdf[:, 0:w_], in_=wd_in.ap()[:, pc * 1024:pc * 1024 + w_]))(pc, w_),
                waits=[t2, ("wqV", P.cnt("wqV"))], inc="dwq")
            t3 = P.add("vector", (lambda w_: lambda v: v.tensor_scalar_mul(
                wdf[:, 0:w_], wdf[:, 0:w_], wsc[:, 7:8]))(w_),
                waits=[tk_dl], inc="wqV")
            t4 = P.add("vector", (lambda w_: lambda v: v.tensor_copy(
                wdi[:, 0:w_], wdf[:, 0:w_]))(w_), waits=[t3], inc="wqV")
            tk_dq = P.add("vector", (lambda pc, w_: lambda v: v.tensor_copy(
                wdb[:, pc * 1024:pc * 1024 + w_], wdi[:, 0:w_]))(pc, w_),
                waits=[t4], inc="wqV")
        tk_dT = None
        for grp in range(6):
            ntl = 4 if grp < 5 else 2
            tkt = None
            for j in range(ntl):
                ic = grp * 4 + j
                tkt = P.add("tensor", (lambda ic, j: lambda t: t.transpose(
                    eb(ps_e[:])[:, j * 128:(j + 1) * 128],
                    wdb[:, ic * 128:(ic + 1) * 128], ident[:]))(ic, j),
                    waits=[tk_dq] + ([("wqE", P.cnt("wqE"))] if j == 0 else []),
                    inc="wqT")
            tk_dT = P.add("scalar", (lambda grp, ntl: lambda s: s.copy(
                wdtb[:, grp * 512:grp * 512 + ntl * 128],
                eb(ps_e[:])[:, 0:ntl * 128]))(grp, ntl),
                waits=[tkt], inc="wqE")
        tk_dW = P.dma("gpsimd", lambda g: g.dma_start(
            out=AP(agw_in, AG_QD, [[128, 128], [16384, NIC], [1, 128]]),
            in_=AP(wdtb, 0, [[NIC * 128, 128], [128, NIC], [1, 128]])),
            waits=[tk_dT], inc="dagw")
        tkt = P.add("tensor", lambda t: t.transpose(ps_f[0:1, 0:128], wsc[:, 6:7],
                                                    identf[:]),
                    waits=[tk_dm, tk_uS], inc="wqT")
        tk_dS = P.add("scalar", lambda s: s.activation(
            scs8[0:1, 256:384], ps_f[0:1, 0:128], AF.Copy, scale=1.0 / 127.0),
            waits=[tkt], inc="wqE")
        P.dma("gpsimd", lambda g: g.dma_start(
            out=AP(ags_in, 0, [[128, 3], [1, 128]]), in_=scs8[0:3, 0:128]),
            waits=[tk_dS, tk_gS, tk_uS], inc="dscl")
        P.dma("gpsimd", lambda g: g.dma_start(
            out=AP(ags_in, 384, [[128, 3], [1, 128]]), in_=scs8[0:3, 128:256]),
            waits=[tk_dS, tk_gS, tk_uS], inc="dscl")
        tk_s8 = P.dma("gpsimd", lambda g: g.dma_start(
            out=AP(ags_in, 768, [[128, 1], [1, 128]]), in_=scs8[0:1, 256:384]),
            waits=[tk_dS, tk_gS, tk_uS], inc="dscl")
        tk_agw = P.add("gpsimd", lambda g: g.collective_compute(
            "AllGather", AluOpType.bypass, replica_groups=[list(range(NCORE))],
            ins=[agw_in.ap()], outs=[agw_out.ap()]),
            waits=[tk_gW, tk_uW, tk_dW], inc="agc")
        tk_ags = P.add("gpsimd", lambda g: g.collective_compute(
            "AllGather", AluOpType.bypass, replica_groups=[list(range(NCORE))],
            ins=[ags_in.ap()], outs=[ags_out.ap()]),
            waits=[tk_s8], inc="agc")
        tk_sclg = P.dma("gpsimd", lambda g: g.dma_start(
            out=scbuf[0:1, 0:1536],
            in_=AP(ags_out, 0, [[1024, 4], [128, 3], [1, 128]])),
            waits=[tk_ags], inc="dscl")

        P.mark('wq')
        tok_a1 = {}
        tok_val = {}
        tks_xu = {}
        tok_h1 = {}
        tok_h1slot = {}
        tok_r1 = {}
        tok_r1slot = {}
        a1T_after = {}

        for sc in range(NSC):
            if sc == 0:
                tk_halo = None
                for pack in range(4):
                    tk_halo = P.add("vector", (lambda pack: lambda v: v.memset(
                        dispF[:, pack * (64 + SCW): pack * (64 + SCW) + 64], 0.0))(pack),
                        inc="m2V")
            else:
                tk_halo = None
                for pack in range(4):
                    tk_halo = P.add("vector", (lambda pack: lambda v: v.tensor_copy(
                        dispF[:, pack * (64 + SCW): pack * (64 + SCW) + 64],
                        dispF[:, dcol(pack, SCW - 64): dcol(pack, SCW)]))(pack),
                        waits=[tok_a1[sc * 4 - 1], ("gsub", P.cnt("gsub"))],
                        inc="m2V")
            tk_halo_all = ("m2V", P.cnt("m2V"))

            for chl in range(4):
                ch = sc * 4 + chl
                tks_x = []
                for cc in range(8):
                    w0 = [tk_xc]
                    if cc >= 2 or ch > 0:
                        w0.append(("a1T", a1T_after.get((ch, cc - 2),
                                                        a1T_after.get((ch - 1, 7), 0))))
                    tks_x.append(P.dma("sync", (lambda ch, cc: lambda s:
                        s.dma_start_transpose(
                            out=xa1[:, (cc % 2) * TCH:((cc % 2) + 1) * TCH],
                            in_=AP(xbf, (ch * TCH) * C + cc * 128,
                                   [[C, TCH], [1, 128]])))(ch, cc),
                        waits=w0, inc="dxa1a" if cc % 2 == 0 else "dxa1b"))
                tkm = {}
                for cc in range(8):
                    for pack in range(4):
                        w0 = [tks_x[cc]]
                        if cc == 0 and pack == 0:
                            w0 += [tk_setup, ("dotf", P.cnt("dotf")),
                                   ("vwE", P.cnt("vwE")), ("a1E", P.cnt("a1E"))]
                        tkm[pack] = P.add("tensor",
                            (lambda cc, pack: lambda t: t.matmul(
                                psd[pack][:],
                                wcombT[:, cc * 512 + pack * 128: cc * 512 + (pack + 1) * 128],
                                xa1[:, (cc % 2) * TCH:((cc % 2) + 1) * TCH],
                                start=(cc == 0), stop=(cc == 7)))(cc, pack),
                            waits=w0, inc="a1T")
                    a1T_after[(ch, cc)] = P.cnt("a1T")
                tk_e = None
                for pack in range(4):
                    tk_e = P.add("scalar", (lambda chl, pack: lambda s: s.copy(
                        dispF[:, dcol(pack, chl * TCH): dcol(pack, (chl + 1) * TCH)],
                        psd[pack][:]))(chl, pack),
                        waits=[tkm[pack], tk_halo_all,
                               ("gsub", P.cnt("gsub")), ("m2V", P.cnt("m2V"))],
                        inc="a1E")
                tok_a1[ch] = tk_e

            P.mark(f'a1_sc{sc}')
            for pack in range(4):
                tk_m2 = P.add("vector", (lambda pack: lambda v: v.reduce_max(
                    m2t[:], dispF[:, pack * (64 + SCW): pack * (64 + SCW) + 64 + SCW],
                    axis=AX.X, apply_absolute_value=True))(pack),
                    waits=[tok_a1[sc * 4 + 3], tk_cm2], inc="m2V")
                tk_m2b = P.add("vector", lambda v: v.scalar_tensor_tensor(
                    m2t[:], m2t[:], 2.0, cmaxc[:], AluOpType.mult, AluOpType.add),
                    waits=[tk_m2], inc="m2V")
                tk_m2c = P.add("vector", lambda v: v.tensor_copy(m2bb[:], m2t[:]),
                               waits=[tk_m2b, ("bmT", P.cnt("bmT"))], inc="m2V")
                tkb = P.add("tensor", lambda t: t.matmul(
                    ps_e[0:64, 0:1], sqabs[:], m2bb[:], start=True, stop=True),
                    waits=[tk_m2c, tk_sqa, ("e2S", P.cnt("e2S")),
                           ("entE", P.cnt("entE")), ("rbE", P.cnt("rbE")),
                           ("bcE", P.cnt("bcE")), ("stE", P.cnt("stE"))],
                    inc="bmT")
                tk_negb = P.add("scalar", (lambda pack, sc: lambda s: s.activation(
                    negb[:, pack * NSC + sc: pack * NSC + sc + 1], ps_e[0:64, 0:1],
                    AF.Copy, scale=-1.02))(pack, sc),
                    waits=[tkb], inc="bmE")

                gctr = sc * 4 * 64 + pack * 64
                for wg in range(4):
                    for ws in range(16):
                        w = 16 * wg + ws
                        gi = gctr + w
                        gso = gi % 2
                        tk_sub = P.add("vector",
                            (lambda pack, w, gso: lambda v: v.tensor_sub(
                                gout[:, gso * SCW:(gso + 1) * SCW],
                                dispF[:, dcol(pack, 0) + w - 63:
                                      dcol(pack, 0) + w - 63 + SCW],
                                dispF[:, dcol(pack, 0): dcol(pack, SCW)]))(pack, w, gso),
                            waits=[tok_a1[sc * 4 + 3],
                                   ("gfree", max(0, P.cnt("gfree") - 1))],
                            inc="gsub")
                        tk_gel = P.add("scalar",
                            (lambda w, gso: lambda s: s.activation(
                                gout[:, gso * SCW:(gso + 1) * SCW],
                                gout[:, gso * SCW:(gso + 1) * SCW],
                                AF.Gelu, bias=ctile[:, w:w + 1]))(w, gso),
                            waits=[tk_sub],
                            inc="gelu")
                        for sub in range(4):
                            wdep = [tk_gel] if sub == 0 else []
                            if ws == 0 and sub == 0:
                                wdep.append(("dotf", P.cnt("dotf")))
                            P.add("tensor",
                                (lambda ws, sub, gso: lambda t: t.matmul(
                                    psd[sub][:], sqbuf[:, 15 - ws: 143 - ws],
                                    gout[:, gso * SCW + sub * TCH:
                                         gso * SCW + (sub + 1) * TCH],
                                    start=(ws == 0), stop=(ws == 15)))(ws, sub, gso),
                                waits=wdep,
                                inc="dotr" if ws == 15 else None,
                                inc2="gfree" if sub == 3 else None)
                    for sub in range(4):
                        dslp = sub % 2
                        tk_sg = P.add("scalar",
                            (lambda sub, dslp: lambda s: s.activation(
                                dsl[64:128, dslp * TCH:(dslp + 1) * TCH],
                                psd[sub][64:128, :], AF.Sigmoid, bias=dmgbc[:]))(sub, dslp),
                            waits=[("dotr", P.cnt("dotr") - (3 - sub)), tk_dbc,
                                   ("relT_", max(0, P.cnt("relT_") - 1))],
                            inc="sigS")
                        tk_e1 = P.add("scalar",
                            (lambda sub, pack, sc: lambda s: s.activation(
                                e1t[:], psd[sub][0:64, :], AF.Exp,
                                bias=negb[:, pack * NSC + sc: pack * NSC + sc + 1]))(
                                sub, pack, sc),
                            waits=[tk_negb, ("ecmb", P.cnt("ecmb"))],
                            inc="sigS", inc2="dotf")
                        tk_rl = P.add("tensor",
                            (lambda dslp: lambda t: t.matmul(
                                ps_e[0:64, :], prel[:],
                                dsl[:, dslp * TCH:(dslp + 1) * TCH],
                                start=True, stop=True))(dslp),
                            waits=[tk_sg, ("e2S", P.cnt("e2S")),
                                   ("bmE", P.cnt("bmE")),
                                   ("entE", P.cnt("entE")),
                                   ("rbE", P.cnt("rbE")),
                                   ("rrV", P.cnt("rrV"))],
                            inc="relT_")
                        tk_e2 = P.add("scalar", lambda s: s.activation(
                            e2t[:], ps_e[0:64, :], AF.Exp, scale=-10.0),
                            waits=[tk_rl], inc="e2S")
                        tk_cb = P.add("vector",
                            (lambda wg, sub: lambda v: v.tensor_mul(
                                etile[:, (wg * 4 + sub) * TCH:(wg * 4 + sub + 1) * TCH],
                                e1t[:], e2t[:]))(wg, sub),
                            waits=[tk_e1, tk_e2, ("entT", P.cnt("entT")),
                                   ("smmT", P.cnt("smmT"))],
                            inc="ecmb")
                        if sc == 0 and sub == 0:
                            P.add("vector", (lambda wg: lambda v: v.tensor_mul(
                                etile[:, (wg * 4) * TCH:(wg * 4 + 1) * TCH],
                                etile[:, (wg * 4) * TCH:(wg * 4 + 1) * TCH],
                                cmask[:, wg * TCH:(wg + 1) * TCH]))(wg),
                                waits=[tk_cb, tk_cons], inc="ecmb")
                P.mark(f'win_sc{sc}_p{pack}')
                tok_ecmb_all = ("ecmb", P.cnt("ecmb"))
                for sub in range(4):
                    tks = None
                    for wg in range(4):
                        tks = P.add("tensor", (lambda wg, sub: lambda t: t.matmul(
                            ps_e[0:4, :], ind4[:],
                            etile[:, (wg * 4 + sub) * TCH:(wg * 4 + sub + 1) * TCH],
                            start=(wg == 0), stop=(wg == 3)))(wg, sub),
                            waits=[tok_ecmb_all] +
                                  ([("rrV", P.cnt("rrV")),
                                    ("rbE", P.cnt("rbE")),
                                    ("e2S", P.cnt("e2S"))] if wg == 0 else []),
                            inc="smmT")
                    tk_rr = P.add("vector", lambda v: v.reciprocal(rrow[:], ps_e[0:4, :]),
                                  waits=[tks], inc="rrV")
                    tk_rrb = P.add("vector", lambda v: v.tensor_copy(rrowb[:], rrow[:]),
                                   waits=[tk_rr], inc="rrV")
                    tk_rb = P.add("tensor", lambda t: t.matmul(
                        ps_e[0:64, :], sel4[:], rrowb[:], start=True, stop=True),
                        waits=[tk_rrb], inc="rbT")
                    tk_rbe = P.add("scalar", lambda s: s.copy(rbev[:], ps_e[0:64, :]),
                                   waits=[tk_rb], inc="rbE")
                    for wg in range(4):
                        P.add("vector", (lambda wg, sub: lambda v: v.tensor_mul(
                            etile[:, (wg * 4 + sub) * TCH:(wg * 4 + sub + 1) * TCH],
                            etile[:, (wg * 4 + sub) * TCH:(wg * 4 + sub + 1) * TCH],
                            rbev[:]))(wg, sub),
                            waits=[tk_rbe], inc="enV")
                tok_en_all = ("enV", P.cnt("enV"))
                for qtl in range(16):
                    qt = sc * 16 + qtl
                    sub = qtl // 4
                    eslot = (qt * 4 + pack) % 2
                    tkt = None
                    for wg in range(4):
                        tkt = P.add("tensor", (lambda wg, sub, qtl: lambda t: t.transpose(
                            eb(ps_e[:])[:, wg * 64:(wg + 1) * 64],
                            etile[:, (wg * 4 + sub) * TCH + (qtl % 4) * 128:
                                  (wg * 4 + sub) * TCH + (qtl % 4 + 1) * 128],
                            ident[0:64, 0:64]))(wg, sub, qtl),
                            waits=[tok_en_all] +
                                  ([("rbE", P.cnt("rbE")),
                                    ("entE", P.cnt("entE"))] if wg == 0 else []),
                            inc="entT")
                    tk_ee = P.add("scalar", (lambda eslot: lambda s: s.copy(
                        AP(ent, eslot * 256, [[512, 128], [64, 4], [16, 4], [1, 16]]),
                        AP(ps_e.bitcast(BF16), 0,
                           [[1024, 128], [16, 4], [64, 4], [1, 16]])))(eslot),
                        waits=[tkt, tok_h1slot.get(eslot)], inc="entE")
                    tok_h1[(qt, pack)] = P.dma("gpsimd",
                        (lambda qt, pack, eslot: lambda g: g.dma_start(
                            out=AP(fbuf, fh(pack * 4) + FOFF + (qt * 128) * FROW,
                                   [[FROW, 128], [FHSZ, 4], [1, 64]]),
                            in_=AP(ent, eslot * 256, [[512, 128], [64, 4], [1, 64]])))(
                            qt, pack, eslot),
                        waits=[tk_ee, tk_fz], inc="h1a" if eslot == 0 else "h1b")
                    tok_h1slot[eslot] = tok_h1[(qt, pack)]

            P.mark(f'packs_sc{sc}')
            for chl in range(4):
                ch = sc * 4 + chl
                for qs in range(4):
                    qt = ch * 4 + qs
                    tkv = None
                    for wave in range(2):
                        tks_x2 = []
                        for j in range(4):
                            cc = wave * 4 + j
                            tks_x2.append(P.dma("sync", (lambda ch, cc, j: lambda s:
                                s.dma_start_transpose(
                                    out=xa2[:, j * TCH:(j + 1) * TCH],
                                    in_=AP(xbf, (ch * TCH) * C + cc * 128,
                                           [[C, TCH], [1, 128]])))(ch, cc, j),
                                waits=[tk_xc, ("valT", P.cnt("valT"))], inc="dxa2"))
                        tok_xa2 = ("dxa2", P.cnt("dxa2"))
                        for j in range(4):
                            cc = wave * 4 + j
                            for half in range(2):
                                w0 = []
                                if j == 0 and half == 0:
                                    w0 = [tok_xa2, tk_wvT]
                                    if wave == 0:
                                        w0 += [("valE", P.cnt("valE")),
                                               ("vwT", P.cnt("vwT"))]
                                tkv = P.add("tensor",
                                    (lambda qs, cc, j, half: lambda t: t.matmul(
                                        ps_big[:, half * 512:(half + 1) * 512],
                                        xa2[:, j * TCH + qs * 128:
                                            j * TCH + (qs + 1) * 128],
                                        wvT[:, cc * 1024 + half * 512:
                                            cc * 1024 + (half + 1) * 512],
                                        start=(cc == 0), stop=(cc == 7)))(qs, cc, j, half),
                                    waits=w0, inc="valT")
                    tok_val[qt] = P.add("scalar", (lambda qt: lambda s: s.copy(
                        valr[:, (qt % 5) * 1024:((qt % 5) + 1) * 1024], ps_big[:]))(qt),
                        waits=[tkv], inc="valE")
                for qs in range(4):
                    qt = ch * 4 + qs
                    for h in range(NH):
                        pack = h // 4
                        ebs0 = (2 * (qt * NH + h)) % 4
                        ebs1 = (2 * (qt * NH + h) + 1) % 4
                        deps = [tok_h1[(qt, pack)]]
                        if qt > 0:
                            deps.append(tok_h1[(qt - 1, pack)])
                        tk_b1 = None
                        if qt > 0:
                            tk_b1 = P.dma("sync", (lambda qt, h, ebs0: lambda s:
                                s.dma_start_transpose(
                                    out=ebt[:, ebs0 * 128:(ebs0 + 1) * 128],
                                    in_=AP(fbuf, fh(h) + FOFF + 256 * (qt * 128) - 65,
                                           [[255, 128], [1, 128]])))(qt, h, ebs0),
                                waits=deps + [("vwT", max(0, P.cnt("vwT") - 4))],
                                inc=f"h2s{ebs0}")
                        tk_b2 = P.dma("sync", (lambda qt, h, ebs1: lambda s:
                            s.dma_start_transpose(
                                out=ebt[:, ebs1 * 128:(ebs1 + 1) * 128],
                                in_=AP(fbuf, fh(h) + FOFF + 256 * (qt * 128) + 63,
                                       [[255, 128], [1, 128]])))(qt, h, ebs1),
                            waits=deps + [("vwT", max(0, P.cnt("vwT") - 4))],
                            inc=f"h2s{ebs1}")
                        first = True
                        tkw = None
                        if qt > 0:
                            tkw = P.add("tensor",
                                (lambda qt, h, qs, ebs0: lambda t: t.matmul(
                                    psd[0][64 * (h % 2):64 * (h % 2) + 64,
                                           qs * 128:(qs + 1) * 128],
                                    valr[:, ((qt - 1) % 5) * 1024 + h * 64:
                                         ((qt - 1) % 5) * 1024 + (h + 1) * 64],
                                    ebt[:, ebs0 * 128:(ebs0 + 1) * 128],
                                    start=True, stop=False,
                                    tile_position=(0, 64 * (h % 2))))(qt, h, qs, ebs0),
                                waits=[tk_b1, tok_val[max(0, qt - 1)], tok_val[qt],
                                       ("vwE", P.cnt("vwE"))],
                                inc="vwT")
                            first = False
                        tkw = P.add("tensor",
                            (lambda qt, h, qs, ebs1, first: lambda t: t.matmul(
                                psd[0][64 * (h % 2):64 * (h % 2) + 64,
                                       qs * 128:(qs + 1) * 128],
                                valr[:, (qt % 5) * 1024 + h * 64:
                                     (qt % 5) * 1024 + (h + 1) * 64],
                                ebt[:, ebs1 * 128:(ebs1 + 1) * 128],
                                start=first, stop=True,
                                tile_position=(0, 64 * (h % 2))))(qt, h, qs, ebs1, first),
                            waits=[tk_b2, tok_val[qt]] +
                                  ([("vwE", P.cnt("vwE"))] if first else []),
                            inc="vwT")
                        if h % 2 == 1:
                            P.add("scalar",
                                (lambda h, qs: lambda s: s.copy(
                                    att[:, (h // 2) * TCH + qs * 128:
                                        (h // 2) * TCH + (qs + 1) * 128],
                                    psd[0][:, qs * 128:(qs + 1) * 128]))(h, qs),
                                waits=[tkw, ("cpT", P.cnt("cpT"))], inc="vwE")
                tok_vw = ("vwE", P.cnt("vwE"))
                tok_u = {}
                for cc in range(8):
                    tks_xu[cc % 2] = P.dma("sync", (lambda ch, cc: lambda s:
                        s.dma_start_transpose(
                            out=xur[:, (cc % 2) * TCH:((cc % 2) + 1) * TCH],
                            in_=AP(xbf, (ch * TCH) * C + cc * 128,
                                   [[C, TCH], [1, 128]])))(ch, cc),
                        waits=[tk_xc, tok_u.get(cc - 2)],
                        inc="dxua" if cc % 2 == 0 else "dxub")
                    wslot = cc % 2
                    tk_wl = None
                    for vc in range(8):
                        tk_wl = P.dma("sync", (lambda cc, vc, wslot: lambda s:
                            s.dma_start_transpose(
                                out=wcT2[:, wslot * 1024 + vc * 128:
                                     wslot * 1024 + (vc + 1) * 128],
                                in_=AP(wcbf, cc * 128 * C + vc * 128,
                                       [[C, 128], [1, 128]])))(cc, vc, wslot),
                            waits=[tk_wc, ("cpT", max(0, P.cnt("cpT") - 8))],
                            inc="dwct0" if wslot == 0 else "dwct1")
                    tkc = None
                    for vc in range(8):
                        w0 = [tk_wl] if vc == 0 else []
                        if vc == 0:
                            w0 += [tok_vw, ("uV", P.cnt("uV")),
                                   ("a1E", P.cnt("a1E"))]
                        tkc = P.add("tensor", (lambda cc, vc, wslot: lambda t: t.matmul(
                            ps_f[:],
                            wcT2[:, wslot * 1024 + vc * 128:
                                 wslot * 1024 + (vc + 1) * 128],
                            att[:, vc * TCH:(vc + 1) * TCH],
                            start=(vc == 0), stop=(vc == 7)))(cc, vc, wslot),
                            waits=w0, inc="cpT")
                    tok_u[cc] = P.add("vector", (lambda cc: lambda v: v.tensor_add(
                        ubuf[:, cc * TCH:(cc + 1) * TCH],
                        xur[:, (cc % 2) * TCH:((cc % 2) + 1) * TCH],
                        ps_f[:]))(cc),
                        waits=[tkc, tks_xu[cc % 2], ("stT", P.cnt("stT")),
                               ("sqS", P.cnt("sqS"))], inc="uV")
                tok_u_all = ("uV", P.cnt("uV"))
                tkst = None
                for cc in range(8):
                    w0 = [tok_u_all]
                    if cc == 0:
                        w0 += [("stE", P.cnt("stE")), ("entE", P.cnt("entE")),
                               ("rrV", P.cnt("rrV")), ("e2S", P.cnt("e2S")),
                               ("bcE", P.cnt("bcE"))]
                    tkst = P.add("tensor", (lambda cc: lambda t: t.matmul(
                        ps_e[0:1, 0:TCH], onescolb[:],
                        ubuf[:, cc * TCH:(cc + 1) * TCH],
                        start=(cc == 0), stop=(cc == 7)))(cc),
                        waits=w0, inc="stT")
                tk_se1 = P.add("scalar", lambda s: s.copy(
                    statr[0:1, 0:TCH], ps_e[0:1, 0:TCH]), waits=[tkst], inc="stE")
                tkst = None
                for cc in range(8):
                    tksq = P.add("scalar", (lambda cc: lambda s: s.activation(
                        u2b[:], ubuf[:, cc * TCH:(cc + 1) * TCH], AF.Square))(cc),
                        waits=[tk_se1] if cc == 0 else [("stT", P.cnt("stT"))],
                        inc="sqS")
                    tkst = P.add("tensor", (lambda cc: lambda t: t.matmul(
                        ps_e[0:1, 0:TCH], onescolb[:], u2b[:],
                        start=(cc == 0), stop=(cc == 7)))(cc),
                        waits=[tksq], inc="stT")
                tk_se2 = P.add("scalar", lambda s: s.copy(
                    statr[0:1, TCH:2 * TCH], ps_e[0:1, 0:TCH]), waits=[tkst], inc="stE")
                tk1 = P.add("vector", lambda v: v.tensor_scalar_mul(
                    thin2[0:1, :], statr[0:1, 0:TCH], 1.0 / C), waits=[tk_se1], inc="thV")
                tk2 = P.add("vector", lambda v: v.tensor_scalar_mul(
                    thin1[0:1, :], statr[0:1, TCH:2 * TCH], 1.0 / C),
                    waits=[tk_se2, tk1], inc="thV")
                tk3 = P.add("vector", lambda v: v.tensor_mul(
                    u2b[0:1, 0:TCH], thin2[0:1, :], thin2[0:1, :]), waits=[tk2], inc="thV")
                tk4 = P.add("vector", lambda v: v.tensor_sub(
                    thin1[0:1, :], thin1[0:1, :], u2b[0:1, 0:TCH]), waits=[tk3], inc="thV")
                tk5 = P.add("vector", lambda v: v.tensor_scalar_add(
                    thin1[0:1, :], thin1[0:1, :], 1e-5), waits=[tk4], inc="thV")
                tk6 = P.add("scalar", lambda s: s.activation(
                    thin1[0:1, :], thin1[0:1, :], AF.Sqrt), waits=[tk5], inc="sqS")
                tk7 = P.add("vector", lambda v: v.reciprocal(
                    thin1[0:1, :], thin1[0:1, :]), waits=[tk6], inc="thV")
                tk8 = P.add("vector", lambda v: v.tensor_mul(
                    thin2[0:1, :], thin2[0:1, :], thin1[0:1, :]), waits=[tk7], inc="thV")
                tk9 = P.add("vector", lambda v: v.tensor_copy(
                    thin1b[0:1, :], thin1[0:1, :]), waits=[tk8], inc="thV")
                tk10 = P.add("vector", lambda v: v.tensor_copy(
                    thin2b[0:1, :], thin2[0:1, :]), waits=[tk9], inc="thV")
                tb1 = P.add("tensor", lambda t: t.matmul(
                    ps_e[:], onesrowb[:], thin1b[0:1, :], start=True, stop=True),
                    waits=[tk9, ("bcE", P.cnt("bcE")), ("stE", P.cnt("stE"))], inc="bcT")
                tb1e = P.add("scalar", lambda s: s.copy(bc1s[:], ps_e[:]),
                             waits=[tb1], inc="bcE")
                tb2 = P.add("tensor", lambda t: t.matmul(
                    ps_e[:], onesrowb[:], thin2b[0:1, :], start=True, stop=True),
                    waits=[tk10, tb1e], inc="bcT")
                tb2e = P.add("scalar", lambda s: s.copy(bc2s[:], ps_e[:]),
                             waits=[tb2], inc="bcE")
                for cc in range(8):
                    tkr = P.add("vector", (lambda cc: lambda v: v.tensor_mul(
                        r1t[:, (cc % 2) * TCH:((cc % 2) + 1) * TCH],
                        ubuf[:, cc * TCH:(cc + 1) * TCH], bc1s[:]))(cc),
                        waits=[tb1e, tok_r1slot.get(cc % 2)], inc="r1V")
                    tkr2 = P.add("vector", (lambda cc: lambda v: v.tensor_sub(
                        r1t[:, (cc % 2) * TCH:((cc % 2) + 1) * TCH],
                        r1t[:, (cc % 2) * TCH:((cc % 2) + 1) * TCH], bc2s[:]))(cc),
                        waits=[tb2e, tkr], inc="r1V")
                    tok_r1[(ch, cc)] = P.dma("gpsimd",
                        (lambda ch, cc: lambda g: g.dma_start(
                            out=AP(r1d, cc * 128 * T + ch * TCH, [[T, 128], [1, TCH]]),
                            in_=r1t[:, (cc % 2) * TCH:((cc % 2) + 1) * TCH]))(ch, cc),
                        waits=[tkr2], inc="dr1a" if cc % 2 == 0 else "dr1b2")
                    tok_r1slot[cc % 2] = tok_r1[(ch, cc)]
                    tkxm = P.add("vector", (lambda cc: lambda v: v.reduce_max(
                        xmax8[:, 1:2], r1t[:, (cc % 2) * TCH:((cc % 2) + 1) * TCH],
                        axis=AX.X, apply_absolute_value=True))(cc),
                        waits=[tkr2], inc="r1V")
                    P.add("vector", lambda v: v.tensor_max(
                        xmax8[:, 0:1], xmax8[:, 0:1], xmax8[:, 1:2]),
                        waits=[tkxm], inc="r1V")
                P.mark(f'ch{ch}')

        P.mark('attn')
        tok_r1_all = [("dr1a", P.cnt("dr1a")), ("dr1b2", P.cnt("dr1b2"))]
        tok_r1v_all = ("r1V", P.cnt("r1V"))

        P.add("vector", lambda v: v.memset(xmax8[:, 1:8], 0.0),
              waits=[tok_r1v_all], inc="r1V")
        tk_xm = P.dma("gpsimd", lambda g: g.dma_start(
            out=AP(mx1_in, 0, [[8, 128], [1, 8]]), in_=xmax8[:]),
            waits=[("r1V", P.cnt("r1V"))], inc="dmx1")
        tk_ar1 = P.add("gpsimd", lambda g: g.collective_compute(
            "AllReduce", AluOpType.max, replica_groups=[list(range(NCORE))],
            ins=[mx1_in.ap()], outs=[mx1_out.ap()]),
            waits=[tk_xm, ("agc", 2)], inc="agc")
        tk_mxl = P.dma("gpsimd", lambda g: g.dma_start(
            out=mxrow[0:1, 0:1024], in_=AP(mx1_out, 0, [[1024, 1], [1, 1024]])),
            waits=[tk_ar1], inc="dmx1")
        tk_gx = P.add("vector", lambda v: v.reduce_max(
            thinsc[0:1, 3:4], mxrow[0:1, 0:1024], axis=AX.X), waits=[tk_mxl], inc="mxv")
        tk_sg1 = P.add("scalar", lambda s: s.activation(
            thinsc[0:1, 4:5], thinsc[0:1, 1:2], AF.Sigmoid),
            waits=[tk_sml, tk_gx], inc="scE")
        tk_sx1 = P.add("vector", lambda v: v.tensor_mul(
            thinsc[0:1, 5:6], thinsc[0:1, 3:4], thinsc[0:1, 4:5]),
            waits=[tk_sg1], inc="mxv")
        tk_sx2 = P.add("vector", lambda v: v.tensor_scalar_mul(
            thinsc[0:1, 5:6], thinsc[0:1, 5:6], 1.0 / 127.0), waits=[tk_sx1], inc="mxv")
        tk_sx3 = P.add("vector", lambda v: v.reciprocal(
            thinsc[0:1, 6:7], thinsc[0:1, 5:6]), waits=[tk_sx2], inc="mxv")
        tkb = P.add("tensor", lambda t: t.matmul(
            ps_e[:, 0:1], onesf[:], thinsc[0:1, 6:7], start=True, stop=True),
            waits=[tk_sx3, ("bcE", P.cnt("bcE"))], inc="scT")
        tk_sxb = P.add("scalar", lambda s: s.copy(sxb[:], ps_e[:, 0:1]),
                       waits=[tkb], inc="scE")
        tkb = P.add("tensor", lambda t: t.matmul(
            ps_e[:, 1:2], onesf[:], thinsc[0:1, 5:6], start=True, stop=True),
            waits=[tk_sx3, tk_sxb], inc="scT")
        tk_sxs = P.add("scalar", lambda s: s.copy(sxs[:], ps_e[:, 1:2]),
                       waits=[tkb], inc="scE")
        tkb = None
        tk_prev = tk_sclg
        for grp in range(2):
            n_ic = 12 if grp == 0 else 10
            if grp == 1:
                tk_prev = P.dma("gpsimd", lambda g: g.dma_start(
                    out=scbuf[0:1, 0:1536],
                    in_=AP(ags_out, 4 * 1024, [[1024, 4], [128, 3], [1, 128]])),
                    waits=[tk_ags, ("scT", P.cnt("scT"))], inc="dscl")
            for j in range(n_ic):
                ic = grp * 12 + j
                tkb = P.add("tensor", (lambda ic, j: lambda t: t.matmul(
                    ps_e[:, 2 + ic:3 + ic],
                    scbuf[0:1, j * 128:(j + 1) * 128],
                    onescolb[0:1, 0:1], start=True, stop=True))(ic, j),
                    waits=[tk_prev, tk_sxs] if j == 0 else [], inc="scT")
        tk_swg = P.add("scalar", lambda s: s.copy(swg_c[:], ps_e[:, 2:2 + NIC]),
                       waits=[tkb], inc="scE")
        tkb = None
        tk_prev = None
        for grp in range(2):
            n_ic = 12 if grp == 0 else 10
            off = 384 if grp == 0 else 4 * 1024 + 384
            ncr = 4
            tk_prev = P.dma("gpsimd", (lambda off: lambda g: g.dma_start(
                out=scbuf[0:1, 0:1536],
                in_=AP(ags_out, off, [[1024, 4], [128, 3], [1, 128]])))(off),
                waits=[tk_ags, ("scT", P.cnt("scT"))], inc="dscl")
            for j in range(n_ic):
                ic = grp * 12 + j
                tkb = P.add("tensor", (lambda ic, j: lambda t: t.matmul(
                    ps_e[:, 2 + ic:3 + ic],
                    scbuf[0:1, j * 128:(j + 1) * 128],
                    onescolb[0:1, 0:1], start=True, stop=True))(ic, j),
                    waits=[tk_prev, tk_swg] if j == 0 else [], inc="scT")
        tk_swu = P.add("scalar", lambda s: s.copy(swu_c[:], ps_e[:, 2:2 + NIC]),
                       waits=[tkb], inc="scE")
        tk_scld = P.dma("gpsimd", lambda g: g.dma_start(
            out=scbuf[0:1, 0:1024],
            in_=AP(ags_out, 768, [[1024, 8], [1, 128]])),
            waits=[tk_ags, ("scT", P.cnt("scT"))], inc="dscl")
        tkb = None
        for hc in range(8):
            tkb = P.add("tensor", (lambda hc: lambda t: t.matmul(
                ps_e[:, 2 + hc:3 + hc],
                scbuf[0:1, hc * 128:(hc + 1) * 128],
                onescolb[0:1, 0:1], start=True, stop=True))(hc),
                waits=[tk_scld, tk_swu] if hc == 0 else [], inc="scT")
        tk_swd = P.add("scalar", lambda s: s.copy(swd_c[:], ps_e[:, 2:10]),
                       waits=[tkb], inc="scE")
        tk_sgc = P.add("vector", lambda v: v.tensor_mul(
            sgcomb[:], swg_c[:], sxs[:].broadcast_to((128, NIC))),
            waits=[tk_swg, tk_sxs], inc="scV")

        P.mark('ffnsc')
        tok_h0 = {}
        tok_h0slot = {}
        for ch in range(NCH):
            tok_qx = {}
            for cc in range(8):
                tl = P.dma("gpsimd", (lambda ch, cc: lambda g: g.dma_start(
                    out=r1t[:, (cc % 2) * TCH:((cc % 2) + 1) * TCH],
                    in_=AP(r1d, cc * 128 * T + ch * TCH, [[T, 128], [1, TCH]])))(ch, cc),
                    waits=tok_r1_all + [("qxV", max(0, P.cnt("qxV") - 4))],
                    inc="dq0" if cc % 2 == 0 else "dq1")
                tq1 = P.add("vector", (lambda cc: lambda v: v.tensor_scalar_mul(
                    qxf[:], r1t[:, (cc % 2) * TCH:((cc % 2) + 1) * TCH], sxb[:]))(cc),
                    waits=[tl, tk_sxb, ("guT", P.cnt("guT"))], inc="qxV")
                tq2 = P.add("vector", lambda v: v.tensor_copy(qxi[:], qxf[:]),
                            waits=[tq1], inc="qxV")
                tq3 = P.add("vector", lambda v: v.tensor_copy(qxf[:], qxi[:]),
                            waits=[tq2], inc="qxV")
                tok_qx[cc] = P.add("vector", (lambda cc: lambda v: v.tensor_scalar(
                    qx[:, cc * TCH:(cc + 1) * TCH], qxf[:], 127.0, -128.0,
                    AluOpType.min, AluOpType.max))(cc),
                    waits=[tq3], inc="qxV")
            tok_qx_all = ("qxV", P.cnt("qxV"))
            for ic in range(NIC):
                gslot = ic % 2
                r, loc = ic // 3, ic % 3
                tgw = P.dma("gpsimd", (lambda r, loc, gslot: lambda g: g.dma_start(
                    out=qgt2[:, gslot * 1024:(gslot + 1) * 1024],
                    in_=AP(agw_out, r * AG_SZ + AG_QG + loc * 128,
                           [[384, 128], [128 * 384, 8], [1, 128]])))(r, loc, gslot),
                    waits=[("agc", 1), ("guT", max(0, P.cnt("guT") - 17))],
                    inc="dgw0" if gslot == 0 else "dgw1")
                tuw = P.dma("gpsimd", (lambda r, loc, gslot: lambda g: g.dma_start(
                    out=qut2[:, gslot * 1024:(gslot + 1) * 1024],
                    in_=AP(agw_out, r * AG_SZ + AG_QU + loc * 128,
                           [[384, 128], [128 * 384, 8], [1, 128]])))(r, loc, gslot),
                    waits=[("agc", 1), ("guT", max(0, P.cnt("guT") - 16))],
                    inc="dgu0" if gslot == 0 else "dgu1")
                gb = ic % 2
                tkg = None
                for hc in range(8):
                    w0 = [tgw, tok_qx_all] if hc == 0 else []
                    if hc == 0:
                        w0.append(("silS", max(0, P.cnt("silS") - 1)))
                    tkg = P.add("tensor", (lambda hc, gslot, gb: lambda t: t.matmul(
                        psd[gb][:],
                        qgt2[:, gslot * 1024 + hc * 128: gslot * 1024 + (hc + 1) * 128],
                        qx[:, hc * TCH:(hc + 1) * TCH],
                        start=(hc == 0), stop=(hc == 7)))(hc, gslot, gb),
                        waits=w0, inc="guT")
                tku = None
                for hc in range(8):
                    w0 = [tuw] if hc == 0 else []
                    if hc == 0:
                        w0.append(("h0V", max(0, P.cnt("h0V") - 1)))
                    tku = P.add("tensor", (lambda hc, gslot, gb: lambda t: t.matmul(
                        psd[2 + gb][:],
                        qut2[:, gslot * 1024 + hc * 128: gslot * 1024 + (hc + 1) * 128],
                        qx[:, hc * TCH:(hc + 1) * TCH],
                        start=(hc == 0), stop=(hc == 7)))(hc, gslot, gb),
                        waits=w0, inc="guT")
                tk_sil = P.add("scalar", (lambda ic, gb: lambda s: s.activation(
                    silu_t[:], psd[gb][:], AF.Silu,
                    scale=sgcomb[:, ic:ic + 1]))(ic, gb),
                    waits=[tkg, tk_sgc, ("h0V", P.cnt("h0V"))], inc="silS")
                hslot = ic % 2
                tk_h0 = P.add("vector", (lambda hslot, gb: lambda v: v.tensor_mul(
                    h0t[:, hslot * TCH:(hslot + 1) * TCH], silu_t[:],
                    psd[2 + gb][:]))(hslot, gb),
                    waits=[tku, tk_sil, tok_h0slot.get(hslot)],
                    inc="h0V")
                tk_hm = P.add("vector", (lambda hslot: lambda v: v.reduce_max(
                    hm1[:], h0t[:, hslot * TCH:(hslot + 1) * TCH],
                    axis=AX.X, apply_absolute_value=True))(hslot),
                    waits=[tk_h0], inc="hmV")
                P.add("vector", (lambda ic: lambda v: v.tensor_max(
                    hmax[:, ic:ic + 1], hmax[:, ic:ic + 1], hm1[:]))(ic),
                    waits=[tk_hm], inc="hmV")
                tok_h0[(ch, ic)] = P.dma("gpsimd",
                    (lambda ch, ic, hslot: lambda g: g.dma_start(
                        out=AP(h0d, ic * 128 * T + ch * TCH, [[T, 128], [1, TCH]]),
                        in_=h0t[:, hslot * TCH:(hslot + 1) * TCH]))(ch, ic, hslot),
                    waits=[tk_h0], inc="dh0a" if hslot == 0 else "dh0b")
                tok_h0slot[hslot] = tok_h0[(ch, ic)]

        P.mark('gateup')
        tk_hs = P.add("vector", lambda v: v.tensor_mul(
            hmax[:], hmax[:], swu_c[:]), waits=[("hmV", P.cnt("hmV")), tk_swu],
            inc="hmV")
        tk_hs2 = P.add("vector", lambda v: v.tensor_mul(
            hmax[:], hmax[:], sxs[:].broadcast_to((128, NIC))),
            waits=[tk_hs], inc="hmV")
        tk_hr = P.add("vector", lambda v: v.reduce_max(
            hmx8[:, 0:1], hmax[:], axis=AX.X), waits=[tk_hs2], inc="hmV")
        tk_hm8 = P.dma("gpsimd", lambda g: g.dma_start(
            out=AP(mx2_in, 0, [[8, 128], [1, 8]]), in_=hmx8[:]),
            waits=[tk_hr], inc="dmx2")
        tk_ar2 = P.add("gpsimd", lambda g: g.collective_compute(
            "AllReduce", AluOpType.max, replica_groups=[list(range(NCORE))],
            ins=[mx2_in.ap()], outs=[mx2_out.ap()]),
            waits=[tk_hm8], inc="agc")
        tk_mxl2 = P.dma("gpsimd", lambda g: g.dma_start(
            out=mxrow[0:1, 0:1024], in_=AP(mx2_out, 0, [[1024, 1], [1, 1024]])),
            waits=[tk_ar2], inc="dmx2")
        tk_gh = P.add("vector", lambda v: v.reduce_max(
            thinsc[0:1, 8:9], mxrow[0:1, 0:1024], axis=AX.X), waits=[tk_mxl2], inc="mxv")
        tk_sg2 = P.add("scalar", lambda s: s.activation(
            thinsc[0:1, 9:10], thinsc[0:1, 2:3], AF.Sigmoid),
            waits=[tk_gh], inc="scE")
        tk_sh1 = P.add("vector", lambda v: v.tensor_mul(
            thinsc[0:1, 10:11], thinsc[0:1, 8:9], thinsc[0:1, 9:10]),
            waits=[tk_sg2], inc="mxv")
        tk_sh2 = P.add("vector", lambda v: v.tensor_scalar_mul(
            thinsc[0:1, 10:11], thinsc[0:1, 10:11], 1.0 / 127.0),
            waits=[tk_sh1], inc="mxv")
        tk_sh3 = P.add("vector", lambda v: v.reciprocal(
            thinsc[0:1, 11:12], thinsc[0:1, 10:11]), waits=[tk_sh2], inc="mxv")
        tkb = P.add("tensor", lambda t: t.matmul(
            ps_e[:, 0:1], onesf[:], thinsc[0:1, 11:12], start=True, stop=True),
            waits=[tk_sh3, ("scE", P.cnt("scE"))], inc="scT")
        tk_shb = P.add("scalar", lambda s: s.copy(shb[:], ps_e[:, 0:1]),
                       waits=[tkb], inc="scE")
        tkb = P.add("tensor", lambda t: t.matmul(
            ps_e[:, 1:2], onesf[:], thinsc[0:1, 10:11], start=True, stop=True),
            waits=[tk_sh3, tk_shb], inc="scT")
        tk_shs = P.add("scalar", lambda s: s.copy(shs[:], ps_e[:, 1:2]),
                       waits=[tkb], inc="scE")
        t1 = P.add("vector", lambda v: v.tensor_mul(
            squcomb[:], swu_c[:], sxs[:].broadcast_to((128, NIC))),
            waits=[tk_swu, tk_sxs], inc="scV")
        tk_squ = P.add("vector", lambda v: v.tensor_mul(
            squcomb[:], squcomb[:], shb[:].broadcast_to((128, NIC))),
            waits=[t1, tk_shb], inc="scV")
        tk_sdc = P.add("vector", lambda v: v.tensor_mul(
            sdcomb[:], swd_c[:], shs[:].broadcast_to((128, 8))),
            waits=[tk_swd, tk_shs], inc="scV")

        P.mark('bar2')
        HT = 256
        for ch in range(NCH):
            for half in range(2):
                tok_qh = {}
                for ic in range(NIC):
                    hslot = ic % 2
                    tl = P.dma("gpsimd", (lambda ch, ic, hslot, half: lambda g: g.dma_start(
                        out=h0t[:, hslot * TCH:hslot * TCH + HT],
                        in_=AP(h0d, ic * 128 * T + ch * TCH + half * HT,
                               [[T, 128], [1, HT]])))(ch, ic, hslot, half),
                        waits=[("dh0a", P.cnt("dh0a")), ("dh0b", P.cnt("dh0b")),
                               ("qhV", max(0, P.cnt("qhV") - 4))],
                        inc="dh0ra" if hslot == 0 else "dh0rb")
                    qs_ = (ic % 2) * HT
                    tq1 = P.add("vector", (lambda ic, hslot, qs_: lambda v: v.tensor_scalar_mul(
                        h0q[:, qs_:qs_ + HT], h0t[:, hslot * TCH:hslot * TCH + HT],
                        squcomb[:, ic:ic + 1]))(ic, hslot, qs_),
                        waits=[tl, tk_squ, ("dnT", P.cnt("dnT"))], inc="qhV")
                    tq2 = P.add("vector", (lambda qs_: lambda v: v.tensor_copy(
                        h0qi[:, qs_:qs_ + HT], h0q[:, qs_:qs_ + HT]))(qs_),
                        waits=[tq1], inc="qhV")
                    tq3 = P.add("vector", (lambda qs_: lambda v: v.tensor_copy(
                        h0q[:, qs_:qs_ + HT], h0qi[:, qs_:qs_ + HT]))(qs_),
                        waits=[tq2], inc="qhV")
                    tok_qh[ic] = P.add("vector", (lambda ic, qs_: lambda v: v.tensor_scalar(
                        qh[:, ic * HT:(ic + 1) * HT], h0q[:, qs_:qs_ + HT], 127.0, -128.0,
                        AluOpType.min, AluOpType.max))(ic, qs_),
                        waits=[tq3], inc="qhV")
                tok_qh_all = ("qhV", P.cnt("qhV"))
                for hc in range(8):
                    dslot2 = hc % 2
                    tdw = P.dma("gpsimd", (lambda hc, dslot2: lambda g: g.dma_start(
                        out=qdt2[:, dslot2 * NIC * 128:(dslot2 + 1) * NIC * 128],
                        in_=AP(agw_out, hc * AG_SZ + AG_QD,
                               [[128, 128], [16384, NIC], [1, 128]])))(hc, dslot2),
                        waits=[("agc", 1), ("dnT", max(0, P.cnt("dnT") - NIC))],
                        inc="ddw0" if dslot2 == 0 else "ddw1")
                    db_ = hc % 2
                    tkd = None
                    for ic in range(NIC):
                        w0 = [tdw, tok_qh_all] if ic == 0 else []
                        if ic == 0:
                            w0.append(("dnE", max(0, P.cnt("dnE") - 3)))
                        tkd = P.add("tensor", (lambda ic, dslot2, db_: lambda t: t.matmul(
                            psd[db_][:, 0:HT],
                            qdt2[:, dslot2 * NIC * 128 + ic * 128:
                                 dslot2 * NIC * 128 + (ic + 1) * 128],
                            qh[:, ic * HT:(ic + 1) * HT],
                            start=(ic == 0), stop=(ic == NIC - 1)))(ic, dslot2, db_),
                            waits=w0, inc="dnT")
                    tl = P.dma("gpsimd", (lambda ch, hc, half: lambda g: g.dma_start(
                        out=r1t[:, (hc % 2) * TCH:(hc % 2) * TCH + HT],
                        in_=AP(r1d, hc * 128 * T + ch * TCH + half * HT,
                               [[T, 128], [1, HT]])))(ch, hc, half),
                        waits=tok_r1_all + [("dnE", max(0, P.cnt("dnE") - 1))],
                        inc="dq0" if hc % 2 == 0 else "dq1")
                    tsc = P.add("vector", (lambda hc, db_: lambda v: v.tensor_scalar_mul(
                        h0q[:, 0:HT], psd[db_][:, 0:HT],
                        sdcomb[:, hc:hc + 1]))(hc, db_),
                        waits=[tkd, tk_sdc, ("qhV", P.cnt("qhV"))], inc="dnE")
                    P.add("vector", (lambda hc, half: lambda v: v.tensor_add(
                        ubuf[:, hc * TCH + half * HT: hc * TCH + (half + 1) * HT],
                        r1t[:, (hc % 2) * TCH:(hc % 2) * TCH + HT],
                        h0q[:, 0:HT]))(hc, half),
                        waits=[tsc, tl, ("stT", P.cnt("stT")), ("sqS", P.cnt("sqS"))],
                        inc="dnE")
            tok_u2_all = ("dnE", P.cnt("dnE"))
            tkst = None
            for cc in range(8):
                w0 = [tok_u2_all]
                if cc == 0:
                    w0 += [("stE", P.cnt("stE")), ("scE", P.cnt("scE")),
                           ("bcE", P.cnt("bcE"))]
                tkst = P.add("tensor", (lambda cc: lambda t: t.matmul(
                    ps_e[0:1, 0:TCH], onescolb[:],
                    ubuf[:, cc * TCH:(cc + 1) * TCH],
                    start=(cc == 0), stop=(cc == 7)))(cc), waits=w0, inc="stT")
            tk_se1 = P.add("scalar", lambda s: s.copy(
                statr[0:1, 0:TCH], ps_e[0:1, 0:TCH]), waits=[tkst], inc="stE")
            tkst = None
            for cc in range(8):
                tksq = P.add("scalar", (lambda cc: lambda s: s.activation(
                    u2b[:], ubuf[:, cc * TCH:(cc + 1) * TCH], AF.Square))(cc),
                    waits=[tk_se1] if cc == 0 else [("stT", P.cnt("stT"))], inc="sqS")
                tkst = P.add("tensor", (lambda cc: lambda t: t.matmul(
                    ps_e[0:1, 0:TCH], onescolb[:], u2b[:],
                    start=(cc == 0), stop=(cc == 7)))(cc), waits=[tksq], inc="stT")
            tk_se2 = P.add("scalar", lambda s: s.copy(
                statr[0:1, TCH:2 * TCH], ps_e[0:1, 0:TCH]), waits=[tkst], inc="stE")
            tk1 = P.add("vector", lambda v: v.tensor_scalar_mul(
                thin2[0:1, :], statr[0:1, 0:TCH], 1.0 / C), waits=[tk_se1], inc="thV")
            tk2 = P.add("vector", lambda v: v.tensor_scalar_mul(
                thin1[0:1, :], statr[0:1, TCH:2 * TCH], 1.0 / C),
                waits=[tk_se2, tk1], inc="thV")
            tk3 = P.add("vector", lambda v: v.tensor_mul(
                u2b[0:1, 0:TCH], thin2[0:1, :], thin2[0:1, :]), waits=[tk2], inc="thV")
            tk4 = P.add("vector", lambda v: v.tensor_sub(
                thin1[0:1, :], thin1[0:1, :], u2b[0:1, 0:TCH]), waits=[tk3], inc="thV")
            tk5 = P.add("vector", lambda v: v.tensor_scalar_add(
                thin1[0:1, :], thin1[0:1, :], 1e-5), waits=[tk4], inc="thV")
            tk6 = P.add("scalar", lambda s: s.activation(
                thin1[0:1, :], thin1[0:1, :], AF.Sqrt), waits=[tk5], inc="sqS")
            tk7 = P.add("vector", lambda v: v.reciprocal(
                thin1[0:1, :], thin1[0:1, :]), waits=[tk6], inc="thV")
            tk8 = P.add("vector", lambda v: v.tensor_mul(
                thin2[0:1, :], thin2[0:1, :], thin1[0:1, :]), waits=[tk7], inc="thV")
            tk9 = P.add("vector", lambda v: v.tensor_copy(
                thin1b[0:1, :], thin1[0:1, :]), waits=[tk8], inc="thV")
            tk10 = P.add("vector", lambda v: v.tensor_copy(
                thin2b[0:1, :], thin2[0:1, :]), waits=[tk9], inc="thV")
            tb1 = P.add("tensor", lambda t: t.matmul(
                ps_e[:], onesrowb[:], thin1b[0:1, :], start=True, stop=True),
                waits=[tk9, ("bcE", P.cnt("bcE")), ("stE", P.cnt("stE"))], inc="bcT")
            tb1e = P.add("scalar", lambda s: s.copy(bc1s[:], ps_e[:]),
                         waits=[tb1], inc="bcE")
            tb2 = P.add("tensor", lambda t: t.matmul(
                ps_e[:], onesrowb[:], thin2b[0:1, :], start=True, stop=True),
                waits=[tk10, tb1e], inc="bcT")
            tb2e = P.add("scalar", lambda s: s.copy(bc2s[:], ps_e[:]),
                         waits=[tb2], inc="bcE")
            for cc in range(8):
                tkr = P.add("vector", (lambda cc: lambda v: v.tensor_mul(
                    o2t[:, 0:TCH], ubuf[:, cc * TCH:(cc + 1) * TCH], bc1s[:]))(cc),
                    waits=[tb1e, ("dob", P.cnt("dob"))], inc="o2V")
                tkr2 = P.add("vector", lambda v: v.tensor_sub(o2t[:, 0:TCH], o2t[:, 0:TCH], bc2s[:]),
                             waits=[tb2e, tkr], inc="o2V")
                P.dma("gpsimd", (lambda ch, cc: lambda g: g.dma_start(
                    out=AP(obf, cc * 128 * T + ch * TCH, [[T, 128], [1, TCH]]),
                    in_=o2t[:, 0:TCH]))(ch, cc),
                    waits=[tkr2], inc="dob")
        tok_obf_all = ("dob", P.cnt("dob"))

        P.mark('down')
        for ch in range(NCH):
            for tt in range(4):
                for hf2 in range(2):
                    for cc in range(4):
                        ccg = hf2 * 4 + cc
                        P.dma("sync", (lambda ch, tt, ccg, cc: lambda s:
                            s.dma_start_transpose(
                                out=otok[:, cc * 128:(cc + 1) * 128],
                                in_=AP(obf, ccg * 128 * T + ch * TCH + tt * 128,
                                       [[T, 128], [1, 128]])))(ch, tt, ccg, cc),
                            waits=[tok_obf_all, ("ocV", P.cnt("ocV"))], inc="dotr2")
                    tk_oc = P.add("vector", lambda v: v.tensor_copy(ocast[:], otok[:, 0:TCH]),
                                  waits=[("dotr2", P.cnt("dotr2")),
                                         ("dos", P.cnt("dos"))], inc="ocV")
                    P.dma("gpsimd", (lambda ch, tt, hf2: lambda g: g.dma_start(
                        out=out_t.ap()[ch * TCH + tt * 128: ch * TCH + (tt + 1) * 128,
                                       hf2 * 512:(hf2 + 1) * 512],
                        in_=ocast[:]))(ch, tt, hf2),
                        waits=[tk_oc], inc="dos")

        dbgout("dispF", [128, 4 * (64 + SCW)])
        dbgout("etile", [64, 16 * TCH])
        dbgout("att", [128, 8 * TCH])
        dbgout("qh", [128, NIC * 256])
        dbgout("fh0", [T, FROW], BF16)
        if "fh0" in dbg:
            P.dma("gpsimd", lambda g: g.dma_start(
                out=dbg["fh0"].ap(),
                in_=AP(fbuf, FOFF, [[FROW, T], [1, FROW]])),
                waits=[("h1a", P.cnt("h1a")), ("h1b", P.cnt("h1b"))], inc="dbgs")
        if "dispF" in dbg:
            P.dma("gpsimd", lambda g: g.dma_start(
                out=dbg["dispF"].ap(), in_=dispF[:]),
                waits=[("a1E", P.cnt("a1E"))], inc="dbgs")
        if "etile" in dbg:
            P.dma("gpsimd", lambda g: g.dma_start(
                out=dbg["etile"].ap(), in_=etile[:]),
                waits=[("enV", P.cnt("enV"))], inc="dbgs")
        if "att" in dbg:
            P.dma("gpsimd", lambda g: g.dma_start(
                out=dbg["att"].ap(), in_=att[:]),
                waits=[("vwE", P.cnt("vwE"))], inc="dbgs")
        if "qh" in dbg:
            P.dma("gpsimd", lambda g: g.dma_start(
                out=dbg["qh"].ap(), in_=qh[:, 0:NIC * 256]),
                waits=[("qhV", P.cnt("qhV"))], inc="dbgs")

        P.add("gpsimd", lambda g: g.nop(),
              waits=[("dos", P.cnt("dos")), ("dbgs", P.cnt("dbgs"))])
        P.add("sync", lambda s: s.nop(),
              waits=[("dos", P.cnt("dos"))])

        if limit is not None:
            P.ops = P.ops[:limit]
            allw = [(k, v) for k, v in P.count.items()]
            cnt2 = {}
            for _e, _f, _w, _i in P.ops:
                for _s, _a in _i:
                    cnt2[_s] = cnt2.get(_s, 0) + _a
            P.count = cnt2
            for _e, _f, _w, _i in P.ops:
                for wi in range(len(_w)):
                    pass
            newops = []
            for _e, _f, _w, _i in P.ops:
                _w2 = [(s2, v2) for (s2, v2) in _w if cnt2.get(s2, 0) >= v2]
                newops.append((_e, _f, _w2, _i))
            P.ops = newops
            qw = [(k, v) for k, v in cnt2.items()]
            for _eng in ("sync", "vector", "scalar", "tensor", "gpsimd"):
                P.ops.append((_eng, (lambda e: e.nop()), qw, []))
        P.check()
        P.emit(blk, sems)
    return nc, dbg


_NC_CACHE = {}


def _get_nc(debug=()):
    key = tuple(sorted(debug))
    if key not in _NC_CACHE:
        _NC_CACHE[key] = build(debug)
    return _NC_CACHE[key]


def kernel(**inputs):
    x = np.asarray(inputs["L0_attn_x"], np.float32)
    wg = np.asarray(inputs["L0_ffn_w_gate"], np.float32)
    wu = np.asarray(inputs["L0_ffn_w_up"], np.float32)
    wd = np.asarray(inputs["L0_ffn_w_down"], np.float32)
    wg_p = np.ones((NCORE * ISH, C), np.float32)
    wg_p[:FFI] = wg
    wu_p = np.ones((NCORE * ISH, C), np.float32)
    wu_p[:FFI] = wu

    common = {
        "wdisp": np.asarray(inputs["L0_attn_W_disp"], np.float32),
        "wval": np.asarray(inputs["L0_attn_W_val"], np.float32),
        "relpos": np.asarray(inputs["L0_attn_rel_pos_emb"], np.float32),
        "sfw": np.asarray(inputs["L0_attn_strain_fused_w"], np.float32),
        "sfb": np.asarray(inputs["L0_attn_strain_fused_b"], np.float32),
        "ppw": np.asarray(inputs["L0_attn_pos_proj_w"], np.float32),
        "wbond": np.asarray(inputs["L0_attn_bond_out_w"], np.float32),
        "wdmg": np.asarray(inputs["L0_attn_damage_out_w"], np.float32),
        "dmgb": np.asarray(inputs["L0_attn_damage_out_b"], np.float32),
        "wcproj": np.asarray(inputs["L0_attn_c_proj_w"], np.float32),
        "clip_x": np.asarray(inputs["L0_ffn_clip_x"], np.float32),
        "clip_dn": np.asarray(inputs["L0_ffn_clip_down"], np.float32),
    }
    in_maps = []
    for r in range(NCORE):
        m = dict(common)
        m["x"] = np.ascontiguousarray(x[r])
        m["wg_sh"] = np.ascontiguousarray(wg_p[r * ISH:(r + 1) * ISH])
        m["wu_sh"] = np.ascontiguousarray(wu_p[r * ISH:(r + 1) * ISH])
        m["wd_sh"] = np.ascontiguousarray(wd[r * 128:(r + 1) * 128])
        in_maps.append(m)
    nc, _ = _get_nc()
    res = run_bass_kernel_spmd(nc, in_maps, core_ids=list(range(NCORE)))
    return np.stack([np.asarray(res.results[r]["out"]) for r in range(NCORE)])
